# version 19
# baseline (speedup 1.0000x reference)
import math
import numpy as np
import ml_dtypes
import concourse.bass as bass
import concourse.mybir as mybir
from concourse.bass_utils import run_bass_kernel_spmd

F32 = mybir.dt.float32
BF16 = mybir.dt.bfloat16
I8 = mybir.dt.int8
AF = mybir.ActivationFunctionType
ALU = mybir.AluOpType
AX = mybir.AxisListType

B, S, D = 4, 4096, 2048
NCORE = 8
SC = 2048
TILE = 512
NT = SC // TILE
TS = TILE // 128
H, KH, HD = 16, 4, 128
W = 512
INTER = 8192
KC = D // 128
OCI = INTER // 128
SCALE = 1.0 / math.sqrt(HD)
NEG = -1.0e30
INV_D = 1.0 / D
EPS_LN = 1.0e-5
EPS_RMS = 1.0e-6


class Sem:
    def __init__(self, nc, name):
        self._cm = nc.semaphore(name)
        self.h = self._cm.__enter__()
        self.n = 0

    def inc(self, ins, k=1):
        ins.then_inc(self.h, k)
        self.n += k
        return self.n


def build_program():
    nc = bass.Bass("TRN2", target_bir_lowering=False)

    x_in = nc.dram_tensor("x", [SC, D], F32, kind="ExternalInput").ap()
    halo_in = nc.dram_tensor("halo", [W, D], F32, kind="ExternalInput").ap()
    wq_in = nc.dram_tensor("wq", [D, D], F32, kind="ExternalInput").ap()
    wk_in = nc.dram_tensor("wk", [D, KH * HD], F32, kind="ExternalInput").ap()
    wv_in = nc.dram_tensor("wv", [D, KH * HD], F32, kind="ExternalInput").ap()
    wo_in = nc.dram_tensor("wo", [D, D], F32, kind="ExternalInput").ap()
    wg_in = nc.dram_tensor("wg", [D, INTER], I8, kind="ExternalInput").ap()
    wu_in = nc.dram_tensor("wu", [D, INTER], I8, kind="ExternalInput").ap()
    wd_in = nc.dram_tensor("wd", [INTER, D], I8, kind="ExternalInput").ap()
    sgate_in = nc.dram_tensor("sgate", [128, OCI], F32, kind="ExternalInput").ap()
    sup_in = nc.dram_tensor("sup", [128, OCI], F32, kind="ExternalInput").ap()
    sdown_in = nc.dram_tensor("sdown", [128, KC], F32, kind="ExternalInput").ap()
    lnw_in = nc.dram_tensor("lnw", [128, KC], F32, kind="ExternalInput").ap()
    ncs = (SC + W) // 128
    cos_in = nc.dram_tensor("cosb", [ncs, 128, 64], BF16, kind="ExternalInput").ap()
    sin_in = nc.dram_tensor("sinb", [ncs, 128, 64], BF16, kind="ExternalInput").ap()
    mp1_in = nc.dram_tensor("mP1", [128, 128], F32, kind="ExternalInput").ap()
    mp2_in = nc.dram_tensor("mP2", [128, 128], F32, kind="ExternalInput").ap()
    mt0_in = nc.dram_tensor("mT0", [4, 128, 640], F32, kind="ExternalInput").ap()
    id_in = nc.dram_tensor("identb", [128, 128], BF16, kind="ExternalInput").ap()
    out_hbm = nc.dram_tensor("out", [SC, D], F32, kind="ExternalOutput").ap()

    c_wq = nc.dram_tensor("c_wq", [KC, 128, D], BF16).ap()
    c_wkv = nc.dram_tensor("c_wkv", [KC, 128, 1024], BF16).ap()
    c_wo = nc.dram_tensor("c_wo", [KC, 128, KC * 128], BF16).ap()
    c_wg = nc.dram_tensor("c_wg", [OCI, 128, KC * 128], BF16).ap()
    c_wu = nc.dram_tensor("c_wu", [OCI, 128, KC * 128], BF16).ap()
    c_wd = nc.dram_tensor("c_wd", [KC, 128, OCI * 128], BF16).ap()

    from contextlib import ExitStack
    with ExitStack() as _stk:
        def _sb(*args):
            return _stk.enter_context(nc.sbuf_tensor(*args))

        def _ps(*args):
            return _stk.enter_context(nc.psum_tensor(*args))

        xn_fm = _sb("xn_fm", [128, KC, TILE], BF16)
        xn2_fm = _sb("xn2_fm", [128, KC, TILE], BF16)
        q_fm = _sb("q_fm", [128, H, 128 * TS], BF16)
        k_ring = _sb("k_ring", [128, 2, KH, W], BF16)
        v_ring = _sb("v_ring", [128, 2, TS, KH * HD], BF16)
        s_fm = _sb("s_fm", [128, KC, TILE], BF16)
        bigreg = _sb("bigreg", [128, 65536], I8)
        scratch = _sb("scratch", [128, 16384], I8)
        wgblk = _sb("wgblk", [128, 2, KC, 128], BF16)
        wublk = _sb("wublk", [128, KC, 128], BF16)
        wdblk = _sb("wdblk", [128, 48, 128], BF16)
        woblk = _sb("woblk", [128, 2, KC, 128], BF16)
        gate_oc = _sb("gate_oc", [128, 2, TILE], BF16)
        up_oc = _sb("up_oc", [128, 2, TILE], BF16)
        g_oc = _sb("g_oc", [128, 2, TILE], BF16)
        cos_sb = _sb("cos_sb", [128, 8, 64], BF16)
        sin_sb = _sb("sin_sb", [128, 8, 64], BF16)
        kvm = _sb("kvm", [128, 2, 256], F32)
        mP1 = _sb("mP1s", [128, 128], F32)
        mP2 = _sb("mP2s", [128, 128], F32)
        identb = _sb("idents", [128, 128], BF16)
        sgate_t = _sb("sgate_t", [128, OCI], F32)
        sup_t = _sb("sup_t", [128, OCI], F32)
        sdown_t = _sb("sdown_t", [128, KC], F32)
        lnw_t = _sb("lnw_t", [128, KC], F32)
        constv = _sb("constv", [128, 2], F32)
        statv = _sb("statv", [128, 16], F32)
        smaxv = _sb("smaxv", [128, 2, 8], F32)
        pT0 = _ps("pT0", [128, 512], F32)
        pT1 = _ps("pT1", [128, 512], F32)
        PB01 = _ps("PB01", [128, 1024], F32)
        PB23 = _ps("PB23", [128, 1024], F32)
        PB45 = _ps("PB45", [128, 1024], F32)
        PB4, PB5 = PB45[:, 0:512], PB45[:, 512:1024]
        PB0, PB1 = PB01[:, 0:512], PB01[:, 512:1024]
        PB2, PB3 = PB23[:, 0:512], PB23[:, 512:1024]
        pSAB = [PB01, PB23]
        pT = [pT0, pT1]
        Bk = [PB0, PB1, PB2, PB3, PB4, PB5]
        sW = Sem(nc, "sW")
        sX = Sem(nc, "sX")
        sP = Sem(nc, "sP")
        sO = [Sem(nc, "sO0"), Sem(nc, "sO1")]
        sT = Sem(nc, "sT")
        sV = Sem(nc, "sV")
        sS = Sem(nc, "sS")
        sE = Sem(nc, "sE")
        sZV = Sem(nc, "sZV")
        sZS = Sem(nc, "sZS")
        slabS = [Sem(nc, f"sl{i}") for i in range(KC)]
        qslabS = [Sem(nc, f"qsl{i}") for i in range(KC)]
        xS = [Sem(nc, "xs0"), Sem(nc, "xs1")]
        csS = [Sem(nc, "cs0"), Sem(nc, "cs1")]
        woS = [Sem(nc, "wo0"), Sem(nc, "wo1")]
        wgS = [Sem(nc, "wgS0"), Sem(nc, "wgS1")]
        wuS = Sem(nc, "wuS")
        wdS = [Sem(nc, f"wd{i}") for i in range(3)]
        mS = Sem(nc, "mS")
        sZP = Sem(nc, "sZP")
        pinS = [Sem(nc, f"pi{i}") for i in range(8)]
        poutS = [Sem(nc, f"po{i}") for i in range(8)]

        def ve(v, ins):
            sZV.inc(ins)
            v.wait_ge(sZV.h, sZV.n)

        def se(sc, ins):
            sZS.inc(ins)
            sc.wait_ge(sZS.h, sZS.n)

        def carve(buf, off_b, dt, shape):
            esz = {BF16: 2, F32: 4, I8: 1}[dt]
            nbytes = int(np.prod(shape[1:])) * esz
            ap = buf[:, off_b:off_b + nbytes].bitcast(dt)
            if len(shape) == 3:
                ap = ap.rearrange("p (a b) -> p a b", a=shape[1])
            elif len(shape) == 4:
                ap = ap.rearrange("p (a b c) -> p a b c", a=shape[1], b=shape[2])
            assert tuple(ap.shape) == tuple(shape), (ap.shape, shape)
            return ap

        with nc.Block("consts") as blk:
            @blk.vector
            def _(v):
                v.memset(constv[:, 0:1], EPS_LN)
                v.memset(constv[:, 1:2], EPS_RMS)

            @blk.scalar
            def _(sc):
                for dst, src in ((mP1[:, :], mp1_in), (mP2[:, :], mp2_in),
                                 (identb[:, :], id_in), (sgate_t[:, :], sgate_in),
                                 (sup_t[:, :], sup_in), (sdown_t[:, :], sdown_in),
                                 (lnw_t[:, :], lnw_in)):
                    mS.inc(sc.dma_start(dst, src), 16)
                sc.wait_ge(mS.h, mS.n)

        wq_r = wq_in.rearrange("(k p) n -> k p n", p=128)
        wk_r = wk_in.rearrange("(k p) n -> k p n", p=128)
        wv_r = wv_in.rearrange("(k p) n -> k p n", p=128)
        wo_r = wo_in.rearrange("(k p) n -> p k n", p=128)
        wg_r = wg_in.rearrange("(k p) n -> p k n", p=128)
        wu_r = wu_in.rearrange("(k p) n -> p k n", p=128)
        wd_r = wd_in.rearrange("(k p) n -> p k n", p=128)
        pbI = [carve(bigreg, h * 16384, BF16, [128, KC, 512]) for h in range(2)]
        pbI2 = [carve(bigreg, h * 16384, BF16, [128, 8192]) for h in range(2)]
        pbP = [carve(bigreg, 32768 + h * 16384, BF16, [128, 8192]) for h in range(2)]

        stages = []
        for kc in range(KC):
            stages.append(dict(
                kind="qkv",
                src=lambda b, kc=kc: [(pbI2[b][:, 0:2048], wq_r[kc]),
                                      (pbI2[b][:, 2048:2560], wk_r[kc]),
                                      (pbI2[b][:, 2560:3072], wv_r[kc])],
                scale=lnw_t[:, kc:kc + 1],
                outs=lambda b, kc=kc: [(c_wq[kc], pbI2[b][:, 0:2048]),
                                       (c_wkv[kc], pbI2[b][:, 2048:3072])],
                keys=[("wq", kc), ("wkv", kc)]))
        for cg in range(4):
            stages.append(dict(
                kind="perm",
                src=lambda b, cg=cg: [(pbI[b][:, :, :],
                                       wo_r[:, :, cg * 512:(cg + 1) * 512])],
                outs=lambda b, cg=cg: [(c_wo[cg * 4 + ol],
                                        pbP[b][:, ol * 2048:(ol + 1) * 2048])
                                       for ol in range(4)],
                keys=[("wo", cg * 4 + ol) for ol in range(4)]))
        for cg in range(16):
            for nm, srcr, dstc in (("wg", wg_r, c_wg), ("wu", wu_r, c_wu)):
                stages.append(dict(
                    kind="perm",
                    src=lambda b, cg=cg, srcr=srcr: [(pbI[b][:, :, :],
                                                      srcr[:, :, cg * 512:(cg + 1) * 512])],
                    outs=lambda b, cg=cg, dstc=dstc: [(dstc[cg * 4 + ol],
                                                       pbP[b][:, ol * 2048:(ol + 1) * 2048])
                                                      for ol in range(4)],
                    keys=[(nm, cg * 4 + ol) for ol in range(4)]))
        for kh in range(4):
            for cg in range(4):
                stages.append(dict(
                    kind="perm",
                    src=lambda b, kh=kh, cg=cg: [(pbI[b][:, :, :],
                                                  wd_r[:, kh * 16:(kh + 1) * 16,
                                                       cg * 512:(cg + 1) * 512])],
                    outs=lambda b, kh=kh, cg=cg: [
                        (c_wd[cg * 4 + ol][:, kh * 16 * 128:(kh + 1) * 16 * 128],
                         pbP[b][:, ol * 2048:(ol + 1) * 2048]) for ol in range(4)],
                    keys=[("wd", cg * 4 + ol, kh) for ol in range(4)]))
        NSTAGE = len(stages)
        piece_ready = {}
        out_totals = {}
        in_totals = {}
        _psim = [0, 0]
        for _s, _st in enumerate(stages):
            _psim[_s % 2] += 16 * len(_st["outs"](0))
            out_totals[_s] = _psim[_s % 2]

        with nc.Block("prepass") as blk:
            @blk.gpsimd
            def _(g):
                for s, st in enumerate(stages):
                    bb = s % 2
                    if s >= 2:
                        if stages[s - 2]["kind"] == "qkv":
                            g.wait_ge(poutS[(s - 2) % 2].h, out_totals[s - 2])
                        else:
                            g.wait_ge(sZP.h, s - 1)
                    inS = pinS[s % 2]
                    for dst, srcap in st["src"](bb):
                        inS.inc(g.dma_start(dst, srcap), 16)
                    in_totals[s] = inS.n

            @blk.vector
            def _(v):
                for s, st in enumerate(stages):
                    bb = s % 2
                    v.wait_ge(pinS[s % 2].h, in_totals[s])
                    if st["kind"] == "qkv":
                        sZP.inc(v.tensor_scalar(pbI2[bb][:, 0:3072], pbI2[bb][:, 0:3072],
                                                st["scale"], None, ALU.mult))
                    else:
                        if s >= 2:
                            v.wait_ge(poutS[s % 2].h, out_totals[s - 2])
                        for ol in range(4):
                            ins = v.tensor_copy(
                                pbP[bb][:, ol * 2048:(ol + 1) * 2048].rearrange(
                                    "p (a b) -> p a b", a=KC),
                                pbI[bb][:, :, ol * 128:(ol + 1) * 128])
                        sZP.inc(ins)

            @blk.sync
            def _(s_eng):
                for s, st in enumerate(stages):
                    bb = s % 2
                    ouS = poutS[s % 2]
                    s_eng.wait_ge(sZP.h, s + 1)
                    for dst, srcv in st["outs"](bb):
                        ouS.inc(s_eng.dma_start(dst, srcv), 16)
                    assert ouS.n == out_totals[s], (s, ouS.n, out_totals[s])
                    for key in st["keys"]:
                        piece_ready[key] = (ouS, ouS.n)

        def wait_piece(eng, key):
            sem, val = piece_ready[key]
            eng.wait_ge(sem.h, val)

        x_r = x_in.rearrange("(n p) d -> n p d", p=128)
        halo_r = halo_in.rearrange("(n p) d -> n p d", p=128)
        out_r = out_hbm.rearrange("(n p) d -> n p d", p=128)

        xslot = [carve(scratch, 0, F32, [128, 2048]), carve(scratch, 8192, F32, [128, 2048])]
        sfm_flat = s_fm[:, :, :].rearrange("p a b -> p (a b)")
        xn_tm = sfm_flat[:, 0:2048]
        xn2_tm = sfm_flat[:, 2048:4096]
        sqdump = PB45
        qtm = sfm_flat[:, 4096:6144]
        qrot = sfm_flat[:, 6144:8192]
        ropeA = gate_oc[:, :, :].rearrange("p a b -> p (a b)").bitcast(F32)
        ropeB = up_oc[:, :, :].rearrange("p a b -> p (a b)").bitcast(F32)
        ktm = g_oc[:, 0, :]
        krot = g_oc[:, 1, :]
        kmA = kvm[:, 0, :]
        kmB = kvm[:, 1, :]

        wq_slab = [carve(bigreg, i * 4096, BF16, [128, 2048]) for i in range(KC)]
        _kvhost = ([wdblk[:, :, :].rearrange("p a b -> p (a b)")[:, i * 1024:(i + 1) * 1024]
                    for i in range(6)] +
                   [woblk[:, :, :, :].rearrange("p a b c -> p (a b c)")[:, i * 1024:(i + 1) * 1024]
                    for i in range(4)] +
                   [wgblk[:, :, :, :].rearrange("p a b c -> p (a b c)")[:, i * 1024:(i + 1) * 1024]
                    for i in range(4)] +
                   [wublk[:, :, :].rearrange("p a b -> p (a b)")[:, i * 1024:(i + 1) * 1024]
                    for i in range(2)])
        wkv_slab = _kvhost
        attnout = carve(bigreg, 0, BF16, [128, KC, TILE])
        probs_n = [[carve(bigreg, 16384 + (p * 4 + h) * 1280, BF16, [128, 640])
                    for h in range(4)] for p in range(2)]
        probsT = [carve(bigreg, 26624 + p * 5120, BF16, [128, 5, 4, 128]) for p in range(2)]
        mT0 = carve(bigreg, 36864, F32, [128, 4, 640])
        fuse = carve(bigreg, 0, BF16, [128, OCI, TILE])
        sg_fm = xn_fm
        out_stage = [
            xn2_fm[:, 0:8, :].bitcast(F32).rearrange("p a b -> p (a b)"),
            xn2_fm[:, 8:16, :].bitcast(F32).rearrange("p a b -> p (a b)"),
        ]
        halo_xn2 = q_fm[:, :, :]

        CSUM, CNM, CVAR, CVA, CVB, CL1, CR1, CIVE, CM2, CL2, CC2, CR2 = range(12)

        def sv(c):
            return statv[:, c:c + 1]

        def smx(par, c):
            return smaxv[:, par, c:c + 1]

        for t in range(NT):
            cur = t % 2
            prv = (t - 1) % 2

            subs = ([("halo", hs) for hs in range(4)] if t == 0 else []) + \
                   [("main", ts) for ts in range(TS)]
            with nc.Block(f"NQK{t}") as blk:
                v0, s0, t0_, e0 = sV.n, sS.n, sT.n, sE.n
                o_guard = (sO[0].n, sO[1].n)
                xtot, stot, cstot = {}, {}, {}
                plan = {}
                Gev = []
                vc, scc, tc, ec = v0, s0, t0_, e0
                nsub = len(subs)
                vorder = []
                for i in range(nsub):
                    vorder.append(("N", i))
                    if i >= 1:
                        vorder.append(("QKV", i - 1))
                vorder.append(("QKV", nsub - 1))
                for i, (kind, idx) in enumerate(subs):
                    key = ("h", idx) if kind == "halo" else idx
                    scc += 1; plan[("ssq", i)] = scc
                    scc += 1; plan[("s1", i)] = scc
                    scc += 1; plan[("s2", i)] = scc
                    nb = 4 if kind == "halo" else 8
                    plan[("nb", i)] = nb
                    for g in range(nb):
                        tc += 1; plan[("Ntr", i, g)] = tc
                        ec += 1; plan[("Ne", i, g)] = ec
                        plan[("NtrG", i, g)] = len(Gev); Gev.append(ec)
                    if kind == "main":
                        ts = idx
                        tc += 1; plan[("Qmm", ts)] = tc
                    tc += 1; plan[("KVmm", key)] = tc
                    if kind == "main":
                        for g in range(4):
                            tc += 1; plan[("Qtr", ts, g)] = tc
                            ec += 1; plan[("Qe", ts, g)] = ec
                            plan[("QtrG", ts, g)] = len(Gev); Gev.append(ec)
                    tc += 1; plan[("KVtr", key)] = tc
                    ec += 1; plan[("Ke", key)] = ec
                    plan[("KVtrG", key)] = len(Gev); Gev.append(ec)
                for ev, i in vorder:
                    kind, idx = subs[i]
                    key = ("h", idx) if kind == "halo" else idx
                    if ev == "N":
                        vc += 1; plan[("v1", i)] = vc
                        vc += 1; plan[("vsq", i)] = vc
                        vc += 1; plan[("v2", i)] = vc
                        vc += 1; plan[("v3", i)] = vc
                    else:
                        if kind == "main":
                            vc += 1; plan[("vqe", idx)] = vc
                            vc += 1; plan[("vqr", idx)] = vc
                        vc += 1; plan[("vke", key)] = vc
                        vc += 1; plan[("vkr", key)] = vc

                def ptg(eng, Gidx):
                    if Gidx >= 2:
                        eng.wait_ge(sE.h, Gev[Gidx - 2])

                @blk.sync
                def _(s):
                    if t == 0:
                        for _j in range(4):
                            if poutS[_j].n:
                                s.wait_ge(poutS[_j].h, poutS[_j].n)
                    def xload(i):
                        kind, idx = subs[i]
                        srcx = halo_r[idx] if kind == "halo" else x_r[t * TS + idx]
                        xS[i % 2].inc(s.dma_start(xslot[i % 2], srcx), 16)
                        xtot[i] = xS[i % 2].n
                    xload(0)
                    xload(1)
                    nr = 8 if t == 0 else 4
                    r0_ = 0 if t == 0 else 4 + t * TS
                    sl = 0 if t == 0 else 4
                    csS[0].inc(s.dma_start(cos_sb[:, sl:sl + nr, :],
                                           cos_in[r0_:r0_ + nr].rearrange("n p f -> p n f")), 16)
                    cstot[0] = csS[0].n
                    csS[1].inc(s.dma_start(sin_sb[:, sl:sl + nr, :],
                                           sin_in[r0_:r0_ + nr].rearrange("n p f -> p n f")), 16)
                    cstot[1] = csS[1].n
                    for kc in range(KC):
                        if t == 0:
                            wait_piece(s, ("wkv", kc))
                        slabS[kc].inc(s.dma_start(wkv_slab[kc], c_wkv[kc]), 16)
                        stot[("kv", kc)] = slabS[kc].n
                    for kc in range(KC):
                        if t == 0:
                            wait_piece(s, ("wq", kc))
                        qslabS[kc].inc(s.dma_start(wq_slab[kc], c_wq[kc]), 16)
                        stot[("q", kc)] = qslabS[kc].n
                    for i in range(2, len(subs)):
                        s.wait_ge(sV.h, plan[("v3", i - 2)])
                        xload(i)

                @blk.tensor
                def _(te):
                    for i, (kind, idx) in enumerate(subs):
                        key = ("h", idx) if kind == "halo" else idx
                        nb = plan[("nb", i)]
                        for g in range(nb):
                            ptg(te, plan[("NtrG", i, g)])
                            if g == 0:
                                te.wait_ge(sV.h, plan[("v3" if kind == "halo" else "v2", i)])
                            if kind == "main" and g == 4:
                                te.wait_ge(sV.h, plan[("v3", i)])
                            src_tm = xn_tm if (kind == "main" and g < 4) else xn2_tm
                            ch0 = (g % 4) * 4
                            for ci in range(4):
                                ch = ch0 + ci
                                ins = te.matmul(pT[plan[("NtrG", i, g)] % 2][:, ci * 128:(ci + 1) * 128],
                                                src_tm[:, ch * 128:(ch + 1) * 128],
                                                identb[:, :], start=True, stop=True)
                            sT.inc(ins)
                        if kind == "main":
                            ts = idx
                            te.wait_ge(sE.h, plan[("Ne", i, nb - 1)])
                            if i >= 1:
                                pk, pi = subs[i - 1]
                                te.wait_ge(sV.h, plan[("vke", ("h", pi) if pk == "halo" else pi)])
                            elif t > 0:
                                pass
                            for kc in range(KC):
                                te.wait_ge(qslabS[kc].h, stot[("q", kc)])
                                lhsT = xn2_fm[:, kc, ts * 128:(ts + 1) * 128]
                                for dc in range(4):
                                    ins = te.matmul(Bk[dc][:, :], lhsT,
                                                    wq_slab[kc][:, dc * 512:(dc + 1) * 512],
                                                    start=(kc == 0), stop=(kc == KC - 1))
                            sT.inc(ins)
                            te.wait_ge(sV.h, plan[("vqe", ts)])
                            for kc in range(KC):
                                te.wait_ge(slabS[kc].h, stot[("kv", kc)])
                                lhsT = xn2_fm[:, kc, ts * 128:(ts + 1) * 128]
                                te.matmul(Bk[0][:, :], lhsT, wkv_slab[kc][:, 0:512],
                                          start=(kc == 0), stop=(kc == KC - 1))
                                ins = te.matmul(Bk[1][:, :], lhsT, wkv_slab[kc][:, 512:1024],
                                                start=(kc == 0), stop=(kc == KC - 1))
                            sT.inc(ins)
                            te.wait_ge(sV.h, plan[("vqr", ts)])
                            for g in range(4):
                                ptg(te, plan[("QtrG", ts, g)])
                                for hi in range(4):
                                    h = g * 4 + hi
                                    ins = te.matmul(pT[plan[("QtrG", ts, g)] % 2][:, hi * 128:(hi + 1) * 128],
                                                    qrot[:, h * 128:(h + 1) * 128],
                                                    identb[:, :], start=True, stop=True)
                                sT.inc(ins)
                        else:
                            te.wait_ge(sE.h, plan[("Ne", i, nb - 1)])
                            if i >= 1:
                                te.wait_ge(sV.h, plan[("vke", ("h", idx - 1))])
                            for kc in range(KC):
                                te.wait_ge(slabS[kc].h, stot[("kv", kc)])
                                lhsT = halo_xn2[:, kc, idx * 128:(idx + 1) * 128]
                                te.matmul(Bk[0][:, :], lhsT, wkv_slab[kc][:, 0:512],
                                          start=(kc == 0), stop=(kc == KC - 1))
                                ins = te.matmul(Bk[1][:, :], lhsT, wkv_slab[kc][:, 512:1024],
                                                start=(kc == 0), stop=(kc == KC - 1))
                            sT.inc(ins)
                        te.wait_ge(sV.h, plan[("vkr", key)])
                        ptg(te, plan[("KVtrG", key)])
                        for ci in range(4):
                            ins = te.matmul(pT[plan[("KVtrG", key)] % 2][:, ci * 128:(ci + 1) * 128],
                                            krot[ci * 128:(ci + 1) * 128] if False else
                                            krot[:, ci * 128:(ci + 1) * 128],
                                            identb[:, :], start=True, stop=True)
                        sT.inc(ins)
                    assert sT.n == tc

                @blk.vector
                def _(v):
                    v.wait_ge(csS[0].h, cstot[0])
                    v.wait_ge(csS[1].h, cstot[1])

                    def nchain(i):
                        kind, idx = subs[i]
                        xb = xslot[i % 2]
                        v.wait_ge(xS[i % 2].h, xtot[i])
                        if i >= 1:
                            v.wait_ge(sS.h, plan[("ssq", i - 1)])
                        ve(v, v.reduce_sum(sv(CSUM), xb, axis=AX.X))
                        sV.inc(v.tensor_scalar(sv(CNM), sv(CSUM), -INV_D, None, ALU.mult))
                        v.wait_ge(sS.h, plan[("ssq", i)])
                        sV.inc(v.tensor_tensor(sv(CVAR), sv(CVA), sv(CVB), ALU.add))
                        v.wait_ge(sS.h, plan[("s1", i)])
                        if kind == "main":
                            v.tensor_scalar(sv(CM2), sv(CIVE), sv(CVAR), INV_D,
                                            ALU.mult, ALU.mult)
                            if i >= 1:
                                v.wait_ge(sT.h, plan[("Ntr", i - 1, plan[("nb", i - 1)] - 1)])
                            sV.inc(v.tensor_scalar(xn_tm[:, :], xb, sv(CNM), sv(CR1),
                                                   ALU.add, ALU.mult))
                        else:
                            sV.inc(v.tensor_scalar(sv(CM2), sv(CIVE), sv(CVAR), INV_D,
                                                   ALU.mult, ALU.mult))
                        v.wait_ge(sS.h, plan[("s2", i)])
                        ve(v, v.tensor_tensor(sv(CR2), sv(CR1), sv(CC2), ALU.mult))
                        if i >= 1:
                            v.wait_ge(sT.h, plan[("Ntr", i - 1, plan[("nb", i - 1)] - 1)])
                        sV.inc(v.tensor_scalar(xn2_tm[:, :], xb, sv(CNM), sv(CR2),
                                               ALU.add, ALU.mult))

                    def qkv(i):
                        kind, idx = subs[i]
                        key = ("h", idx) if kind == "halo" else idx
                        if kind == "main":
                            ts = idx
                            v.wait_ge(sT.h, plan[("Qmm", ts)])
                            for dc in range(4):
                                ins = v.tensor_copy(qtm[:, dc * 512:(dc + 1) * 512], Bk[dc][:, :])
                            sV.inc(ins)
                            v.wait_ge(sV.h, sV.n)
                            qt3 = qtm.rearrange("p (h f) -> p h f", h=H)
                            qr3 = qrot.rearrange("p (h f) -> p h f", h=H)
                            c3 = cos_sb[:, 4 + ts, :].unsqueeze(1).broadcast_to([128, 8, 64])
                            s3 = sin_sb[:, 4 + ts, :].unsqueeze(1).broadcast_to([128, 8, 64])
                            rA = ropeA.rearrange("p (h f) -> p h f", h=8)
                            rB = ropeB.rearrange("p (h f) -> p h f", h=8)
                            for hh in range(2):
                                hs_ = slice(hh * 8, (hh + 1) * 8)
                                v.tensor_tensor(rA, qt3[:, hs_, 0:64], c3, ALU.mult)
                                ve(v, v.tensor_tensor(rB, qt3[:, hs_, 64:128], s3, ALU.mult))
                                ve(v, v.tensor_tensor(qr3[:, hs_, 0:64], rA, rB, ALU.subtract))
                                v.tensor_tensor(rA, qt3[:, hs_, 0:64], s3, ALU.mult)
                                ve(v, v.tensor_tensor(rB, qt3[:, hs_, 64:128], c3, ALU.mult))
                                ins = v.tensor_tensor(qr3[:, hs_, 64:128], rA, rB, ALU.add)
                                if hs_.start == 0:
                                    ve(v, ins)
                            sV.inc(ins)
                        slot_ = cur if kind == "main" else prv
                        csl = (4 + idx) if kind == "main" else idx
                        v.wait_ge(sT.h, plan[("KVmm", key)])
                        if kind == "main" and idx >= 1:
                            v.wait_ge(sT.h, plan[("KVtr", idx - 1)])
                        elif kind == "main" and t == 0:
                            v.wait_ge(sT.h, plan[("KVtr", ("h", 3))])
                        elif kind == "halo" and idx >= 1:
                            v.wait_ge(sT.h, plan[("KVtr", ("h", idx - 1))])
                        v.tensor_copy(ktm, Bk[0][:, :])
                        sV.inc(v.tensor_copy(v_ring[:, slot_, idx, :], Bk[1][:, :]))
                        v.wait_ge(sV.h, sV.n)
                        kt = ktm.rearrange("p (h f) -> p h f", h=KH)
                        kr = krot.rearrange("p (h f) -> p h f", h=KH)
                        c4 = cos_sb[:, csl, :].unsqueeze(1).broadcast_to([128, KH, 64])
                        s4 = sin_sb[:, csl, :].unsqueeze(1).broadcast_to([128, KH, 64])
                        mA_ = kmA.rearrange("p (h f) -> p h f", h=KH)
                        mB_ = kmB.rearrange("p (h f) -> p h f", h=KH)
                        v.tensor_tensor(mA_, kt[:, :, 0:64], c4, ALU.mult)
                        ve(v, v.tensor_tensor(mB_, kt[:, :, 64:128], s4, ALU.mult))
                        ve(v, v.tensor_tensor(kr[:, :, 0:64], mA_, mB_, ALU.subtract))
                        v.tensor_tensor(mA_, kt[:, :, 0:64], s4, ALU.mult)
                        ve(v, v.tensor_tensor(mB_, kt[:, :, 64:128], c4, ALU.mult))
                        sV.inc(v.tensor_tensor(kr[:, :, 64:128], mA_, mB_, ALU.add))

                    for ev, i in vorder:
                        if ev == "N":
                            nchain(i)
                        else:
                            qkv(i)
                    assert sV.n == vc

                @blk.scalar
                def _(sc):
                    if o_guard[0] or o_guard[1]:
                        sc.wait_ge(sO[0].h, o_guard[0])
                        sc.wait_ge(sO[1].h, o_guard[1])

                    def chain1(i):
                        kind, idx = subs[i]
                        sc.wait_ge(sV.h, plan[("v1", i)])
                        se(sc, sc.activation(sqdump[:, :], xslot[i % 2][:, 0:1024],
                                             AF.Square, bias=sv(CNM), accum_out=sv(CVA)))
                        sS.inc(sc.activation(sqdump[:, :], xslot[i % 2][:, 1024:2048],
                                             AF.Square, bias=sv(CNM), accum_out=sv(CVB)))
                        sc.wait_ge(sV.h, plan[("vsq", i)])
                        se(sc, sc.activation(sv(CL1), sv(CVAR), AF.Ln, scale=INV_D,
                                             bias=constv[:, 0:1]))
                        sc.activation(sv(CR1), sv(CL1), AF.Exp, scale=-0.5)
                        sS.inc(sc.activation(sv(CIVE), sv(CL1), AF.Exp, scale=-1.0))

                    def chain2(i):
                        sc.wait_ge(sV.h, plan[("v2", i)])
                        se(sc, sc.activation(sv(CL2), sv(CM2), AF.Ln, bias=constv[:, 1:2]))
                        sS.inc(sc.activation(sv(CC2), sv(CL2), AF.Exp, scale=-0.5))

                    def evacsN(i):
                        kind, idx = subs[i]
                        nb = plan[("nb", i)]
                        for g in range(nb):
                            sc.wait_ge(sT.h, plan[("Ntr", i, g)])
                            ch0 = (g % 4) * 4
                            if kind == "main":
                                dstt = (xn_fm if g < 4 else xn2_fm)[:, ch0:ch0 + 4,
                                                                    idx * 128:(idx + 1) * 128]
                            else:
                                dstt = halo_xn2[:, ch0:ch0 + 4, idx * 128:(idx + 1) * 128]
                            srcp = pT[plan[("NtrG", i, g)] % 2][:, :].rearrange("p (a b) -> p a b", a=4)
                            sE.inc(sc.copy(dstt, srcp))

                    def evacsQK(i):
                        kind, idx = subs[i]
                        key = ("h", idx) if kind == "halo" else idx
                        if kind == "main":
                            for g in range(4):
                                sc.wait_ge(sT.h, plan[("Qtr", idx, g)])
                                dstt = q_fm[:, g * 4:(g + 1) * 4, idx * 128:(idx + 1) * 128]
                                srcp = pT[plan[("QtrG", idx, g)] % 2][:, :].rearrange("p (a b) -> p a b", a=4)
                                sE.inc(sc.copy(dstt, srcp))
                        sc.wait_ge(sT.h, plan[("KVtr", key)])
                        slot_ = cur if kind == "main" else prv
                        dstt = k_ring[:, slot_, :, idx * 128:(idx + 1) * 128]
                        srcp = pT[plan[("KVtrG", key)] % 2][:, :].rearrange("p (a b) -> p a b", a=4)
                        sE.inc(sc.copy(dstt, srcp))

                    for i in range(len(subs)):
                        chain1(i)
                        if i >= 1:
                            evacsN(i - 1)
                        chain2(i)
                        if i >= 1:
                            evacsQK(i - 1)
                    evacsN(len(subs) - 1)
                    evacsQK(len(subs) - 1)
                    assert sS.n == scc and sE.n == ec


            with nc.Block(f"A{t}") as blk:
                t0_, v0, s0 = sT.n, sV.n, sS.n
                mtot = [mS.n]


                def t_scores(a, hu):
                    return t0_ + 12 * a + hu + 1

                def t_trb(a, j):
                    return t0_ + 12 * a + 4 + j + 1

                def t_av(a):
                    return t0_ + 12 * a + 10

                def t_gate(a):
                    return t0_ + 12 * a + 11

                def t_up(a):
                    return t0_ + 12 * a + 12

                def v_bias(a, hu):
                    return v0 + 11 * a + 2 * hu + 1

                def v_norm(a, hu):
                    return v0 + 11 * a + 2 * hu + 2

                def v_aev(a):
                    return v0 + 11 * a + 9

                def v_upe(a):
                    return v0 + 11 * a + 10

                def v_fus(a):
                    return v0 + 11 * a + 11

                def s_exp(a, hu):
                    return s0 + 10 * a + hu + 1

                def s_pev(a, j):
                    return s0 + 10 * a + 4 + j + 1

                def s_gev(a):
                    return s0 + 10 * a + 10

                agtot, autot = {}, {}

                @blk.sync
                def _(s):
                    if t == 0:
                        mS.inc(s.dma_start(mT0[:, :, :],
                                           mt0_in.rearrange("n p f -> p n f")), 16)
                        mtot[0] = mS.n
                    for a in range(16):
                        oc = 48 + a
                        if a >= 2:
                            s.wait_ge(sT.h, t_gate(a - 2))
                        if t == 0:
                            wait_piece(s, ("wg", oc))
                        wgS[oc % 2].inc(s.dma_start(wgblk[:, oc % 2, :, :], c_wg[oc]), 16)
                        agtot[a] = wgS[oc % 2].n
                        if a >= 1:
                            s.wait_ge(sT.h, t_up(a - 1))
                        if t == 0:
                            wait_piece(s, ("wu", oc))
                        wuS.inc(s.dma_start(wublk[:, :, :], c_wu[oc]), 16)
                        autot[a] = wuS.n


                units = [(kg, qt) for kg in range(KH) for qt in range(TS)]

                @blk.tensor
                def _(te):
                    for a, (kg, qt) in enumerate(units):
                        par = a % 2
                        N1 = 512 - qt * 128
                        N2 = 128 + qt * 128
                        for hu in range(4):
                            h = kg * 4 + hu
                            u_abs = 4 * a + hu
                            if u_abs >= 2:
                                prev_a, prev_h = divmod(u_abs - 2, 4)
                                te.wait_ge(sS.h, s_exp(prev_a, prev_h))
                            pp = hu % 2
                            SAB = pSAB[pp]
                            lq = q_fm[:, h, qt * 128:(qt + 1) * 128]
                            te.matmul(SAB[:, 512 - N1:512], lq,
                                      k_ring[:, prv, kg, qt * 128:512],
                                      start=True, stop=True)
                            ins = te.matmul(SAB[:, 512:512 + N2], lq,
                                            k_ring[:, cur, kg, 0:N2],
                                            start=True, stop=True)
                            sT.inc(ins)
                        te.wait_ge(sV.h, v_norm(a, 3))
                        if a >= 1:
                            pass
                        for j in range(5):
                            gb = 5 * a + j
                            if gb >= 2:
                                pa, pj = divmod(gb - 2, 5)
                                te.wait_ge(sS.h, s_pev(pa, pj))
                            for hu in range(4):
                                ins = te.matmul(pT[gb % 2][:, hu * 128:(hu + 1) * 128],
                                                probs_n[par][hu][:, j * 128:(j + 1) * 128],
                                                identb[:, :], start=True, stop=True)
                            sT.inc(ins)
                        te.wait_ge(sS.h, s_pev(a, 4))
                        if a >= 1:
                            te.wait_ge(sV.h, v_aev(a - 1))
                        AVb = Bk[4]
                        for j in range(5):
                            wcol = qt + j
                            slot, st = (prv, wcol) if wcol < 4 else (cur, wcol - 4)
                            ins = te.matmul(AVb[:, :],
                                            v_ring[:, slot, st, kg * 128:(kg + 1) * 128],
                                            probsT[par][:, j, :, :].rearrange("p a b -> p (a b)"),
                                            start=(j == 0), stop=(j == 4))
                        sT.inc(ins)
                        oc = 48 + a
                        te.wait_ge(wgS[oc % 2].h, agtot[a])
                        if a >= 1:
                            te.wait_ge(sV.h, v_upe(a - 1))
                        for kc in range(KC):
                            ins = te.matmul(Bk[5][:, :], wgblk[:, oc % 2, kc, :],
                                            xn_fm[:, kc, :],
                                            start=(kc == 0), stop=(kc == KC - 1))
                        sT.inc(ins)
                        te.wait_ge(wuS.h, autot[a])
                        te.wait_ge(sS.h, s_gev(a))
                        for kc in range(KC):
                            ins = te.matmul(Bk[5][:, :], wublk[:, kc, :],
                                            xn_fm[:, kc, :],
                                            start=(kc == 0), stop=(kc == KC - 1))
                        sT.inc(ins)

                @blk.vector
                def _(v):
                    if t == 0:
                        v.wait_ge(mS.h, mtot[0])
                    for a, (kg, qt) in enumerate(units):
                        par = a % 2
                        N1 = 512 - qt * 128
                        N2 = 128 + qt * 128
                        for hu in range(4):
                            v.wait_ge(sT.h, t_scores(a, hu))
                            pp = hu % 2
                            SAB = pSAB[pp]
                            strip = SAB[:, 512 - N1:512 + N2]
                            if t == 0:
                                ve(v, v.tensor_tensor(strip, strip, mT0[:, qt, :], ALU.add))
                            else:
                                v.tensor_tensor(SAB[:, 512 - N1:512 - N1 + 128],
                                                SAB[:, 512 - N1:512 - N1 + 128],
                                                mP1[:, :], ALU.add)
                                ve(v, v.tensor_tensor(SAB[:, 512 + N2 - 128:512 + N2],
                                                      SAB[:, 512 + N2 - 128:512 + N2],
                                                      mP2[:, :], ALU.add))
                            ve(v, v.reduce_max(smx(pp, 2), strip, axis=AX.X, negate=True))
                            sV.inc(v.tensor_scalar(smx(pp, 3), smx(pp, 2), SCALE,
                                                   None, ALU.mult))
                            v.wait_ge(sS.h, s_exp(a, hu))
                            ve(v, v.reciprocal(smx(pp, 7), smx(pp, 6)))
                            sV.inc(v.tensor_scalar(probs_n[par][hu][:, :],
                                                   probs_n[par][hu][:, :],
                                                   smx(pp, 7), None, ALU.mult))
                        v.wait_ge(sT.h, t_av(a))
                        dstt = attnout[:, kg * 4:(kg + 1) * 4, qt * 128:(qt + 1) * 128]
                        sV.inc(v.tensor_copy(dstt,
                                             Bk[4][:, :].rearrange("p (a b) -> p a b", a=4)))
                        oc = 48 + a
                        v.wait_ge(sT.h, t_up(a))
                        sV.inc(v.tensor_scalar(up_oc[:, a % 2, :], Bk[5][:, :],
                                               sup_t[:, oc:oc + 1], None, ALU.mult))
                        v.wait_ge(sV.h, sV.n)
                        sV.inc(v.tensor_tensor(fuse[:, oc, :], gate_oc[:, a % 2, :],
                                               up_oc[:, a % 2, :], ALU.mult))

                @blk.scalar
                def _(sc):
                    for a, (kg, qt) in enumerate(units):
                        par = a % 2
                        N1 = 512 - qt * 128
                        N2 = 128 + qt * 128
                        for hu in range(4):
                            sc.wait_ge(sV.h, v_bias(a, hu))
                            pp = hu % 2
                            SAB = pSAB[pp]
                            sS.inc(sc.activation(probs_n[par][hu][:, :],
                                                 SAB[:, 512 - N1:512 + N2], AF.Exp,
                                                 bias=smx(pp, 3), scale=SCALE,
                                                 accum_out=smx(pp, 6)))
                        for j in range(5):
                            sc.wait_ge(sT.h, t_trb(a, j))
                            sS.inc(sc.copy(probsT[par][:, j, :, :],
                                                  pT[(5 * a + j) % 2][:, :].rearrange(
                                                      "p (a b) -> p a b", a=4)))
                        oc = 48 + a
                        sc.wait_ge(sT.h, t_gate(a))
                        if a >= 2:
                            sc.wait_ge(sV.h, v_fus(a - 2))
                        sS.inc(sc.activation(gate_oc[:, a % 2, :], Bk[5][:, :],
                                             AF.Silu, scale=sgate_t[:, oc:oc + 1]))

                assert sT.n == t0_ + 192 and sV.n == v0 + 176 and sS.n == s0 + 160

            with nc.Block(f"O{t}") as blk:
                t0_, v0 = sT.n, sV.n
                wotot = {}


                @blk.sync
                def _(s):
                    for oc in range(KC):
                        if oc >= 2:
                            s.wait_ge(sT.h, t0_ + oc - 1)
                        if t == 0:
                            wait_piece(s, ("wo", oc))
                        woS[oc % 2].inc(s.dma_start(woblk[:, oc % 2, :, :], c_wo[oc]), 16)
                        wotot[oc] = woS[oc % 2].n

                @blk.tensor
                def _(te):
                    for oc in range(KC):
                        te.wait_ge(woS[oc % 2].h, wotot[oc])
                        if oc >= 2:
                            te.wait_ge(sV.h, v0 + oc - 1)
                        for kc in range(KC):
                            ins = te.matmul(Bk[oc % 2][:, :], woblk[:, oc % 2, kc, :],
                                            attnout[:, kc, :],
                                            start=(kc == 0), stop=(kc == KC - 1))
                        sT.inc(ins)

                @blk.vector
                def _(v):
                    for oc in range(KC):
                        v.wait_ge(sT.h, t0_ + oc + 1)
                        sV.inc(v.tensor_copy(s_fm[:, oc, :], Bk[oc % 2][:, :]))

            with nc.Block(f"F{t}") as blk:
                t0_, v0, s0 = sT.n, sV.n, sS.n
                wgtot, wutot, wdtot = {}, {}, {}

                OCF = 48
                tg = lambda oc: t0_ + 2 * oc + 1
                tu = lambda oc: t0_ + 2 * oc + 2
                sg_ = lambda oc: s0 + oc + 1
                vu = lambda oc: v0 + 2 * oc + 1
                vf = lambda oc: v0 + 2 * oc + 2
                td = lambda ocd, q: t0_ + 2 * OCF + 4 * ocd + q + 1
                sd_ = lambda ocd: s0 + OCF + ocd + 1
                vsg = lambda ocd: v0 + 2 * OCF + ocd + 1

                @blk.sync
                def _(s):
                    for oc in range(OCF):
                        if oc >= 2:
                            s.wait_ge(sT.h, tg(oc - 2))
                        if t == 0:
                            wait_piece(s, ("wg", oc))
                        wgS[oc % 2].inc(s.dma_start(wgblk[:, oc % 2, :, :], c_wg[oc]), 16)
                        wgtot[oc] = wgS[oc % 2].n
                        if oc >= 1:
                            s.wait_ge(sT.h, tu(oc - 1))
                        if t == 0:
                            wait_piece(s, ("wu", oc))
                        wuS.inc(s.dma_start(wublk[:, :, :], c_wu[oc]), 16)
                        wutot[oc] = wuS.n
                    for ocd in range(KC):
                        for q in range(4):
                            gq = ocd * 4 + q
                            if gq >= 3:
                                pocd, pq = divmod(gq - 3, 4)
                                s.wait_ge(sT.h, td(pocd, pq))
                            if t == 0:
                                wait_piece(s, ("wd", ocd, q))
                            wdS[gq % 3].inc(s.dma_start(wdblk[:, (gq % 3) * 16:(gq % 3 + 1) * 16, :],
                                                        c_wd[ocd][:, q * 16 * 128:(q + 1) * 16 * 128]), 16)
                            wdtot[(ocd, q)] = wdS[gq % 3].n

                @blk.tensor
                def _(te):
                    for oc in range(OCF):
                        te.wait_ge(wgS[oc % 2].h, wgtot[oc])
                        if oc >= 2:
                            te.wait_ge(sS.h, sg_(oc - 2))
                        for kc in range(KC):
                            ins = te.matmul(Bk[oc % 2][:, :], wgblk[:, oc % 2, kc, :],
                                            xn_fm[:, kc, :],
                                            start=(kc == 0), stop=(kc == KC - 1))
                        sT.inc(ins)
                        te.wait_ge(wuS.h, wutot[oc])
                        if oc >= 2:
                            te.wait_ge(sV.h, vu(oc - 2))
                        for kc in range(KC):
                            ins = te.matmul(Bk[2 + oc % 2][:, :], wublk[:, kc, :],
                                            xn_fm[:, kc, :],
                                            start=(kc == 0), stop=(kc == KC - 1))
                        sT.inc(ins)
                    for ocd in range(KC):
                        if ocd >= 2:
                            te.wait_ge(sS.h, sd_(ocd - 2))
                        for q in range(4):
                            gq = ocd * 4 + q
                            te.wait_ge(wdS[gq % 3].h, wdtot[(ocd, q)])
                            if ocd == 0 and q < 3:
                                te.wait_ge(sV.h, vf((q + 1) * 16 - 1))
                            for kc in range(q * 16, (q + 1) * 16):
                                ins = te.matmul(Bk[4 + ocd % 2][:, :],
                                                wdblk[:, (gq % 3) * 16 + (kc - q * 16), :],
                                                fuse[:, kc, :],
                                                start=(kc == 0), stop=(kc == OCI - 1))
                            sT.inc(ins)

                @blk.scalar
                def _(sc):
                    for oc in range(OCF):
                        sc.wait_ge(sT.h, tg(oc))
                        if oc >= 2:
                            sc.wait_ge(sV.h, vf(oc - 2))
                        sS.inc(sc.activation(gate_oc[:, oc % 2, :], Bk[oc % 2][:, :],
                                             AF.Silu, scale=sgate_t[:, oc:oc + 1]))
                    for ocd in range(KC):
                        sc.wait_ge(sT.h, td(ocd, 3))
                        if ocd >= 2:
                            sc.wait_ge(sV.h, vsg(ocd - 2))
                        sS.inc(sc.activation(g_oc[:, ocd % 2, :], Bk[4 + ocd % 2][:, :],
                                             AF.Silu, scale=sdown_t[:, ocd:ocd + 1]))

                @blk.vector
                def _(v):
                    for oc in range(OCF):
                        v.wait_ge(sT.h, tu(oc))
                        sV.inc(v.tensor_scalar(up_oc[:, oc % 2, :], Bk[2 + oc % 2][:, :],
                                               sup_t[:, oc:oc + 1], None, ALU.mult))
                        v.wait_ge(sV.h, sV.n)
                        v.wait_ge(sS.h, sg_(oc))
                        sV.inc(v.tensor_tensor(fuse[:, oc, :], gate_oc[:, oc % 2, :],
                                               up_oc[:, oc % 2, :], ALU.mult))
                    for ocd in range(KC):
                        v.wait_ge(sS.h, sd_(ocd))
                        sV.inc(v.tensor_tensor(sg_fm[:, ocd, :], s_fm[:, ocd, :],
                                               g_oc[:, ocd % 2, :], ALU.mult))

                assert sT.n == t0_ + 2 * OCF + 64 and sS.n == s0 + OCF + 16 and sV.n == v0 + 2 * OCF + 16

            with nc.Block(f"E{t}") as blk:
                t0_, v0 = sT.n, sV.n
                o0 = (sO[0].n, sO[1].n)
                xtot = {}

                @blk.scalar
                def _(sc):
                    for ts in range(TS):
                        if ts >= 2:
                            sc.wait_ge(sV.h, v0 + 4 * (ts - 1))
                        xS[ts % 2].inc(sc.dma_start(xslot[ts % 2], x_r[t * TS + ts]), 16)
                        xtot[ts] = xS[ts % 2].n

                @blk.tensor
                def _(te):
                    for ts in range(TS):
                        for g in range(4):
                            G = 4 * ts + g
                            if G >= 2:
                                te.wait_ge(sV.h, v0 + G - 1)
                            for i in range(4):
                                oc = g * 4 + i
                                ins = te.matmul(pT[G % 2][:, i * 128:(i + 1) * 128],
                                                sg_fm[:, oc, ts * 128:(ts + 1) * 128],
                                                identb[:, :], start=True, stop=True)
                            sT.inc(ins)

                @blk.vector
                def _(v):
                    for ts in range(TS):
                        v.wait_ge(xS[ts % 2].h, xtot[ts])
                        if ts >= 2:
                            v.wait_ge(sO[ts % 2].h, o0[ts % 2] + 16)
                        for g in range(4):
                            G = 4 * ts + g
                            v.wait_ge(sT.h, t0_ + G + 1)
                            sV.inc(v.tensor_tensor(out_stage[ts % 2][:, g * 512:(g + 1) * 512],
                                                   pT[G % 2][:, :],
                                                   xslot[ts % 2][:, g * 512:(g + 1) * 512],
                                                   ALU.add))

                @blk.sync
                def _(s):
                    for ts in range(TS):
                        s.wait_ge(sV.h, v0 + 4 * (ts + 1))
                        sO[ts % 2].inc(s.dma_start(out_r[t * TS + ts], out_stage[ts % 2]), 16)

        with nc.Block("fin") as blk:
            fin = (sO[0].n, sO[1].n)

            @blk.sync
            def _(s):
                s.wait_ge(sO[0].h, fin[0])
                s.wait_ge(sO[1].h, fin[1])

    return nc


_PROG_CACHE = {}


def _get_program():
    if "nc" not in _PROG_CACHE:
        _PROG_CACHE["nc"] = build_program()
    return _PROG_CACHE["nc"]


def _rope_perm(nheads):
    perm = []
    for h in range(nheads):
        base = h * 128
        perm += [base + 2 * i for i in range(64)]
        perm += [base + 2 * i + 1 for i in range(64)]
    return np.array(perm, dtype=np.int64)


def kernel(L0_attn_x, L0_attn_ln_w, L0_attn_wq, L0_attn_wk, L0_attn_wv, L0_attn_wo,
           L0_ffn_w_gate, L0_ffn_w_up, L0_ffn_w_down, L0_ffn_s_gate, L0_ffn_s_up,
           L0_ffn_s_down):
    x = np.ascontiguousarray(np.asarray(L0_attn_x, dtype=np.float32))
    wq = np.ascontiguousarray(np.asarray(L0_attn_wq, dtype=np.float32)[:, _rope_perm(H)])
    wk = np.ascontiguousarray(np.asarray(L0_attn_wk, dtype=np.float32)[:, _rope_perm(KH)])
    wv = np.ascontiguousarray(np.asarray(L0_attn_wv, dtype=np.float32))
    wo = np.ascontiguousarray(np.asarray(L0_attn_wo, dtype=np.float32))
    wg = np.ascontiguousarray(np.asarray(L0_ffn_w_gate, dtype=np.int8))
    wu = np.ascontiguousarray(np.asarray(L0_ffn_w_up, dtype=np.int8))
    wd = np.ascontiguousarray(np.asarray(L0_ffn_w_down, dtype=np.int8))
    sgate = np.ascontiguousarray(np.asarray(L0_ffn_s_gate, dtype=np.float32).reshape(OCI, 128).T)
    sup = np.ascontiguousarray(np.asarray(L0_ffn_s_up, dtype=np.float32).reshape(OCI, 128).T)
    sdown = np.ascontiguousarray(np.asarray(L0_ffn_s_down, dtype=np.float32).reshape(KC, 128).T)
    lnw = np.ascontiguousarray(np.asarray(L0_attn_ln_w, dtype=np.float32).reshape(KC, 128).T)

    qi = np.arange(128)[:, None]
    cc = np.arange(128)[None, :]
    mP1 = np.where(qi <= cc, 0.0, NEG).astype(np.float32)
    mP2 = np.where(cc <= qi, 0.0, NEG).astype(np.float32)
    identm = np.eye(128, dtype=ml_dtypes.bfloat16)

    inv_freq = (1.0 / 10000.0 ** (np.arange(0, HD, 2, dtype=np.float32) / HD)).astype(np.float32)

    def trig_tables(seq_off):
        pos = np.concatenate([np.arange(seq_off - W, seq_off), np.arange(seq_off, seq_off + SC)])
        pos = np.maximum(pos, 0).astype(np.float32)
        ang = pos[:, None] * inv_freq[None, :]
        return (np.cos(ang).astype(ml_dtypes.bfloat16).reshape(-1, 128, 64),
                np.sin(ang).astype(ml_dtypes.bfloat16).reshape(-1, 128, 64))

    def t0_mask(first_half):
        m = np.empty((4, 128, 640), dtype=np.float32)
        for qt in range(4):
            cp = np.arange(640)[None, :]
            valid = (qi <= cp) & (cp <= qi + 512)
            if first_half:
                valid &= (qt * 128 + cp) >= 512
            m[qt] = np.where(valid, 0.0, NEG)
        return m

    nc = _get_program()
    in_maps = []
    for c in range(NCORE):
        b, half = c // 2, c % 2
        off = half * SC
        xc = np.ascontiguousarray(x[b, off:off + SC])
        haloc = (np.zeros((W, D), dtype=np.float32) if half == 0
                 else np.ascontiguousarray(x[b, off - W:off]))
        cosb, sinb = trig_tables(off)
        in_maps.append({
            "x": xc, "halo": haloc, "wq": wq, "wk": wk, "wv": wv, "wo": wo,
            "wg": wg, "wu": wu, "wd": wd, "sgate": sgate, "sup": sup,
            "sdown": sdown, "lnw": lnw, "cosb": cosb, "sinb": sinb,
            "mP1": mP1, "mP2": mP2, "mT0": t0_mask(half == 0), "identb": identm,
        })

    res = run_bass_kernel_spmd(nc, in_maps, core_ids=list(range(NCORE)))
    out = np.empty((B, S, D), dtype=np.float32)
    for c in range(NCORE):
        b, half = c // 2, c % 2
        out[b, half * SC:(half + 1) * SC] = res.results[c]["out"]
    return out


# version 20
# speedup vs baseline: 1.0647x; 1.0647x over previous
import math
import numpy as np
import ml_dtypes
import concourse.bass as bass
import concourse.mybir as mybir
from concourse.bass_utils import run_bass_kernel_spmd

F32 = mybir.dt.float32
BF16 = mybir.dt.bfloat16
I8 = mybir.dt.int8
AF = mybir.ActivationFunctionType
ALU = mybir.AluOpType
AX = mybir.AxisListType

B, S, D = 4, 4096, 2048
NCORE = 8
SC = 2048
TILE = 512
NT = SC // TILE
TS = TILE // 128
H, KH, HD = 16, 4, 128
W = 512
INTER = 8192
KC = D // 128
OCI = INTER // 128
SCALE = 1.0 / math.sqrt(HD)
NEG = -1.0e30
INV_D = 1.0 / D
EPS_LN = 1.0e-5
EPS_RMS = 1.0e-6


class Sem:
    def __init__(self, nc, name):
        self._cm = nc.semaphore(name)
        self.h = self._cm.__enter__()
        self.n = 0

    def inc(self, ins, k=1):
        ins.then_inc(self.h, k)
        self.n += k
        return self.n


def build_program():
    nc = bass.Bass("TRN2", target_bir_lowering=False)

    x_in = nc.dram_tensor("x", [SC, D], F32, kind="ExternalInput").ap()
    halo_in = nc.dram_tensor("halo", [W, D], F32, kind="ExternalInput").ap()
    wq_in = nc.dram_tensor("wq", [D, D], F32, kind="ExternalInput").ap()
    wk_in = nc.dram_tensor("wk", [D, KH * HD], F32, kind="ExternalInput").ap()
    wv_in = nc.dram_tensor("wv", [D, KH * HD], F32, kind="ExternalInput").ap()
    wo_in = nc.dram_tensor("wo", [D, D], F32, kind="ExternalInput").ap()
    wg_in = nc.dram_tensor("wg", [D, INTER], I8, kind="ExternalInput").ap()
    wu_in = nc.dram_tensor("wu", [D, INTER], I8, kind="ExternalInput").ap()
    wd_in = nc.dram_tensor("wd", [INTER, D], I8, kind="ExternalInput").ap()
    sgate_in = nc.dram_tensor("sgate", [128, OCI], F32, kind="ExternalInput").ap()
    sup_in = nc.dram_tensor("sup", [128, OCI], F32, kind="ExternalInput").ap()
    sdown_in = nc.dram_tensor("sdown", [128, KC], F32, kind="ExternalInput").ap()
    lnw_in = nc.dram_tensor("lnw", [128, KC], F32, kind="ExternalInput").ap()
    ncs = (SC + W) // 128
    cos_in = nc.dram_tensor("cosb", [ncs, 128, 64], BF16, kind="ExternalInput").ap()
    sin_in = nc.dram_tensor("sinb", [ncs, 128, 64], BF16, kind="ExternalInput").ap()
    mp1_in = nc.dram_tensor("mP1", [128, 128], F32, kind="ExternalInput").ap()
    mp2_in = nc.dram_tensor("mP2", [128, 128], F32, kind="ExternalInput").ap()
    mt0_in = nc.dram_tensor("mT0", [4, 128, 640], F32, kind="ExternalInput").ap()
    id_in = nc.dram_tensor("identb", [128, 128], BF16, kind="ExternalInput").ap()
    out_hbm = nc.dram_tensor("out", [SC, D], F32, kind="ExternalOutput").ap()

    c_wq = nc.dram_tensor("c_wq", [KC, 128, D], BF16).ap()
    c_wkv = nc.dram_tensor("c_wkv", [KC, 128, 1024], BF16).ap()
    c_wo = nc.dram_tensor("c_wo", [KC, 128, KC * 128], BF16).ap()
    c_wg = nc.dram_tensor("c_wg", [OCI, 128, KC * 128], BF16).ap()
    c_wu = nc.dram_tensor("c_wu", [OCI, 128, KC * 128], BF16).ap()
    c_wd = nc.dram_tensor("c_wd", [KC, 128, OCI * 128], BF16).ap()

    from contextlib import ExitStack
    with ExitStack() as _stk:
        def _sb(*args):
            return _stk.enter_context(nc.sbuf_tensor(*args))

        def _ps(*args):
            return _stk.enter_context(nc.psum_tensor(*args))

        xn_fm = _sb("xn_fm", [128, KC, TILE], BF16)
        xn2_fm = _sb("xn2_fm", [128, KC, TILE], BF16)
        q_fm = _sb("q_fm", [128, H, 128 * TS], BF16)
        k_ring = _sb("k_ring", [128, 2, KH, W], BF16)
        v_ring = _sb("v_ring", [128, 2, TS, KH * HD], BF16)
        s_fm = _sb("s_fm", [128, KC, TILE], BF16)
        bigreg = _sb("bigreg", [128, 65536], I8)
        scratch = _sb("scratch", [128, 16384], I8)
        wgblk = _sb("wgblk", [128, 2, KC, 128], BF16)
        wublk = _sb("wublk", [128, KC, 128], BF16)
        wdblk = _sb("wdblk", [128, 48, 128], BF16)
        woblk = _sb("woblk", [128, 2, KC, 128], BF16)
        gate_oc = _sb("gate_oc", [128, 2, TILE], BF16)
        up_oc = _sb("up_oc", [128, 2, TILE], BF16)
        g_oc = _sb("g_oc", [128, 2, TILE], BF16)
        cos_sb = _sb("cos_sb", [128, 8, 64], BF16)
        sin_sb = _sb("sin_sb", [128, 8, 64], BF16)
        kvm = _sb("kvm", [128, 2, 256], F32)
        mP1 = _sb("mP1s", [128, 128], F32)
        mP2 = _sb("mP2s", [128, 128], F32)
        identb = _sb("idents", [128, 128], BF16)
        sgate_t = _sb("sgate_t", [128, OCI], F32)
        sup_t = _sb("sup_t", [128, OCI], F32)
        sdown_t = _sb("sdown_t", [128, KC], F32)
        lnw_t = _sb("lnw_t", [128, KC], F32)
        constv = _sb("constv", [128, 2], F32)
        statv = _sb("statv", [128, 16], F32)
        smaxv = _sb("smaxv", [128, 2, 8], F32)
        pT0 = _ps("pT0", [128, 512], F32)
        pT1 = _ps("pT1", [128, 512], F32)
        PB01 = _ps("PB01", [128, 1024], F32)
        PB23 = _ps("PB23", [128, 1024], F32)
        PB45 = _ps("PB45", [128, 1024], F32)
        PB4, PB5 = PB45[:, 0:512], PB45[:, 512:1024]
        PB0, PB1 = PB01[:, 0:512], PB01[:, 512:1024]
        PB2, PB3 = PB23[:, 0:512], PB23[:, 512:1024]
        pSAB = [PB01, PB23]
        pT = [pT0, pT1]
        Bk = [PB0, PB1, PB2, PB3, PB4, PB5]
        sW = Sem(nc, "sW")
        sX = Sem(nc, "sX")
        sP = Sem(nc, "sP")
        sO = [Sem(nc, "sO0"), Sem(nc, "sO1")]
        sT = Sem(nc, "sT")
        sV = Sem(nc, "sV")
        sS = Sem(nc, "sS")
        sE = Sem(nc, "sE")
        sZV = Sem(nc, "sZV")
        sZS = Sem(nc, "sZS")
        slabS = [Sem(nc, f"sl{i}") for i in range(KC)]
        qslabS = [Sem(nc, f"qsl{i}") for i in range(KC)]
        xS = [Sem(nc, "xs0"), Sem(nc, "xs1")]
        csS = [Sem(nc, "cs0"), Sem(nc, "cs1")]
        woS = [Sem(nc, "wo0"), Sem(nc, "wo1")]
        wgS = [Sem(nc, "wgS0"), Sem(nc, "wgS1")]
        wuS = Sem(nc, "wuS")
        wdS = [Sem(nc, f"wd{i}") for i in range(3)]
        mS = Sem(nc, "mS")
        sZP = Sem(nc, "sZP")
        pinS = [Sem(nc, f"pi{i}") for i in range(8)]
        poutS = [Sem(nc, f"po{i}") for i in range(8)]

        def ve(v, ins):
            sZV.inc(ins)
            v.wait_ge(sZV.h, sZV.n)

        def se(sc, ins):
            sZS.inc(ins)
            sc.wait_ge(sZS.h, sZS.n)

        def carve(buf, off_b, dt, shape):
            esz = {BF16: 2, F32: 4, I8: 1}[dt]
            nbytes = int(np.prod(shape[1:])) * esz
            ap = buf[:, off_b:off_b + nbytes].bitcast(dt)
            if len(shape) == 3:
                ap = ap.rearrange("p (a b) -> p a b", a=shape[1])
            elif len(shape) == 4:
                ap = ap.rearrange("p (a b c) -> p a b c", a=shape[1], b=shape[2])
            assert tuple(ap.shape) == tuple(shape), (ap.shape, shape)
            return ap

        with nc.Block("consts") as blk:
            @blk.vector
            def _(v):
                v.memset(constv[:, 0:1], EPS_LN)
                v.memset(constv[:, 1:2], EPS_RMS)

            @blk.scalar
            def _(sc):
                for dst, src in ((mP1[:, :], mp1_in), (mP2[:, :], mp2_in),
                                 (identb[:, :], id_in), (sgate_t[:, :], sgate_in),
                                 (sup_t[:, :], sup_in), (sdown_t[:, :], sdown_in),
                                 (lnw_t[:, :], lnw_in)):
                    mS.inc(sc.dma_start(dst, src), 16)
                sc.wait_ge(mS.h, mS.n)

        wq_r = wq_in.rearrange("(k p) n -> k p n", p=128)
        wk_r = wk_in.rearrange("(k p) n -> k p n", p=128)
        wv_r = wv_in.rearrange("(k p) n -> k p n", p=128)
        wo_r = wo_in.rearrange("(k p) n -> p k n", p=128)
        wg_r = wg_in.rearrange("(k p) n -> p k n", p=128)
        wu_r = wu_in.rearrange("(k p) n -> p k n", p=128)
        wd_r = wd_in.rearrange("(k p) n -> p k n", p=128)
        pbI = [carve(bigreg, h * 16384, BF16, [128, KC, 512]) for h in range(2)]
        pbI2 = [carve(bigreg, h * 16384, BF16, [128, 8192]) for h in range(2)]
        pbP = [carve(bigreg, 32768 + h * 16384, BF16, [128, 8192]) for h in range(2)]

        stages = []
        for kc in range(KC):
            stages.append(dict(
                kind="qkv",
                src=lambda b, kc=kc: [(pbI2[b][:, 0:2048], wq_r[kc]),
                                      (pbI2[b][:, 2048:2560], wk_r[kc]),
                                      (pbI2[b][:, 2560:3072], wv_r[kc])],
                scale=lnw_t[:, kc:kc + 1],
                outs=lambda b, kc=kc: [(c_wq[kc], pbI2[b][:, 0:2048]),
                                       (c_wkv[kc], pbI2[b][:, 2048:3072])],
                keys=[("wq", kc), ("wkv", kc)]))
        for cg in range(4):
            stages.append(dict(
                kind="perm",
                src=lambda b, cg=cg: [(pbI[b][:, :, :],
                                       wo_r[:, :, cg * 512:(cg + 1) * 512])],
                outs=lambda b, cg=cg: [(c_wo[cg * 4 + ol],
                                        pbP[b][:, ol * 2048:(ol + 1) * 2048])
                                       for ol in range(4)],
                keys=[("wo", cg * 4 + ol) for ol in range(4)]))
        for cg in range(16):
            for nm, srcr, dstc in (("wg", wg_r, c_wg), ("wu", wu_r, c_wu)):
                stages.append(dict(
                    kind="perm",
                    src=lambda b, cg=cg, srcr=srcr: [(pbI[b][:, :, :],
                                                      srcr[:, :, cg * 512:(cg + 1) * 512])],
                    outs=lambda b, cg=cg, dstc=dstc: [(dstc[cg * 4 + ol],
                                                       pbP[b][:, ol * 2048:(ol + 1) * 2048])
                                                      for ol in range(4)],
                    keys=[(nm, cg * 4 + ol) for ol in range(4)]))
        for kh in range(4):
            for cg in range(4):
                stages.append(dict(
                    kind="perm",
                    src=lambda b, kh=kh, cg=cg: [(pbI[b][:, :, :],
                                                  wd_r[:, kh * 16:(kh + 1) * 16,
                                                       cg * 512:(cg + 1) * 512])],
                    outs=lambda b, kh=kh, cg=cg: [
                        (c_wd[cg * 4 + ol][:, kh * 16 * 128:(kh + 1) * 16 * 128],
                         pbP[b][:, ol * 2048:(ol + 1) * 2048]) for ol in range(4)],
                    keys=[("wd", cg * 4 + ol, kh) for ol in range(4)]))
        NSTAGE = len(stages)
        piece_ready = {}
        out_totals = {}
        in_totals = {}
        _psim = [0, 0]
        for _s, _st in enumerate(stages):
            _psim[_s % 2] += 16 * len(_st["outs"](0))
            out_totals[_s] = _psim[_s % 2]

        with nc.Block("prepass") as blk:
            @blk.gpsimd
            def _(g):
                for s, st in enumerate(stages):
                    bb = s % 2
                    if s >= 2:
                        if stages[s - 2]["kind"] == "qkv":
                            g.wait_ge(poutS[(s - 2) % 2].h, out_totals[s - 2])
                        else:
                            g.wait_ge(sZP.h, s - 1)
                    inS = pinS[s % 2]
                    for dst, srcap in st["src"](bb):
                        inS.inc(g.dma_start(dst, srcap), 16)
                    in_totals[s] = inS.n

            @blk.vector
            def _(v):
                for s, st in enumerate(stages):
                    bb = s % 2
                    v.wait_ge(pinS[s % 2].h, in_totals[s])
                    if st["kind"] == "qkv":
                        sZP.inc(v.tensor_scalar(pbI2[bb][:, 0:3072], pbI2[bb][:, 0:3072],
                                                st["scale"], None, ALU.mult))
                    else:
                        if s >= 2:
                            v.wait_ge(poutS[s % 2].h, out_totals[s - 2])
                        for ol in range(4):
                            ins = v.tensor_copy(
                                pbP[bb][:, ol * 2048:(ol + 1) * 2048].rearrange(
                                    "p (a b) -> p a b", a=KC),
                                pbI[bb][:, :, ol * 128:(ol + 1) * 128])
                        sZP.inc(ins)

            @blk.sync
            def _(s_eng):
                for s, st in enumerate(stages):
                    bb = s % 2
                    ouS = poutS[s % 2]
                    s_eng.wait_ge(sZP.h, s + 1)
                    for dst, srcv in st["outs"](bb):
                        ouS.inc(s_eng.dma_start(dst, srcv), 16)
                    assert ouS.n == out_totals[s], (s, ouS.n, out_totals[s])
                    for key in st["keys"]:
                        piece_ready[key] = (ouS, ouS.n)

        def wait_piece(eng, key):
            sem, val = piece_ready[key]
            eng.wait_ge(sem.h, val)

        x_r = x_in.rearrange("(n p) d -> n p d", p=128)
        halo_r = halo_in.rearrange("(n p) d -> n p d", p=128)
        out_r = out_hbm.rearrange("(n p) d -> n p d", p=128)

        xslot = [carve(scratch, 0, F32, [128, 2048]), carve(scratch, 8192, F32, [128, 2048])]
        sfm_flat = s_fm[:, :, :].rearrange("p a b -> p (a b)")
        xn_tm = sfm_flat[:, 0:2048]
        xn2_tm = sfm_flat[:, 2048:4096]
        sqdump = PB45
        qtm = sfm_flat[:, 4096:6144]
        qrot = sfm_flat[:, 6144:8192]
        ropeA = gate_oc[:, :, :].rearrange("p a b -> p (a b)").bitcast(F32)
        ropeB = up_oc[:, :, :].rearrange("p a b -> p (a b)").bitcast(F32)
        ktm = g_oc[:, 0, :]
        krot = g_oc[:, 1, :]
        kmA = kvm[:, 0, :]
        kmB = kvm[:, 1, :]

        wq_slab = [carve(bigreg, i * 4096, BF16, [128, 2048]) for i in range(KC)]
        _kvhost = ([wdblk[:, :, :].rearrange("p a b -> p (a b)")[:, i * 1024:(i + 1) * 1024]
                    for i in range(6)] +
                   [woblk[:, :, :, :].rearrange("p a b c -> p (a b c)")[:, i * 1024:(i + 1) * 1024]
                    for i in range(4)] +
                   [wgblk[:, :, :, :].rearrange("p a b c -> p (a b c)")[:, i * 1024:(i + 1) * 1024]
                    for i in range(4)] +
                   [wublk[:, :, :].rearrange("p a b -> p (a b)")[:, i * 1024:(i + 1) * 1024]
                    for i in range(2)])
        wkv_slab = _kvhost
        attnout = carve(bigreg, 0, BF16, [128, KC, TILE])
        probs_n = [[carve(bigreg, 16384 + (p * 4 + h) * 1280, BF16, [128, 640])
                    for h in range(4)] for p in range(2)]
        probsT = [carve(bigreg, 26624 + p * 5120, BF16, [128, 5, 4, 128]) for p in range(2)]
        mT0 = carve(bigreg, 36864, F32, [128, 4, 640])
        fuse = carve(bigreg, 0, BF16, [128, OCI, TILE])
        sg_fm = xn_fm
        out_stage = [
            xn2_fm[:, 0:8, :].bitcast(F32).rearrange("p a b -> p (a b)"),
            xn2_fm[:, 8:16, :].bitcast(F32).rearrange("p a b -> p (a b)"),
        ]
        halo_xn2 = q_fm[:, :, :]

        CSUM, CNM, CVAR, CVA, CVB, CL1, CR1, CIVE, CM2, CL2, CC2, CR2 = range(12)

        def sv(c):
            return statv[:, c:c + 1]

        def smx(par, c):
            return smaxv[:, par, c:c + 1]

        for t in range(NT):
            cur = t % 2
            prv = (t - 1) % 2

            subs = ([("halo", hs) for hs in range(4)] if t == 0 else []) + \
                   [("main", ts) for ts in range(TS)]
            with nc.Block(f"NQK{t}") as blk:
                v0, s0, t0_, e0 = sV.n, sS.n, sT.n, sE.n
                o_guard = (sO[0].n, sO[1].n)
                xtot, stot, cstot = {}, {}, {}
                plan = {}
                Gev = []
                vc, scc, tc, ec = v0, s0, t0_, e0
                nsub = len(subs)
                vorder = []
                for i in range(nsub):
                    vorder.append(("N", i))
                    if i >= 1:
                        vorder.append(("QKV", i - 1))
                vorder.append(("QKV", nsub - 1))
                for i, (kind, idx) in enumerate(subs):
                    key = ("h", idx) if kind == "halo" else idx
                    scc += 1; plan[("ssq", i)] = scc
                    scc += 1; plan[("s1", i)] = scc
                    scc += 1; plan[("s2", i)] = scc
                    nb = 4 if kind == "halo" else 8
                    plan[("nb", i)] = nb
                    for g in range(nb):
                        tc += 1; plan[("Ntr", i, g)] = tc
                        ec += 1; plan[("Ne", i, g)] = ec
                        plan[("NtrG", i, g)] = len(Gev); Gev.append(ec)
                    if kind == "main":
                        ts = idx
                        tc += 1; plan[("Qmm", ts)] = tc
                    tc += 1; plan[("KVmm", key)] = tc
                    if kind == "main":
                        for g in range(4):
                            tc += 1; plan[("Qtr", ts, g)] = tc
                            ec += 1; plan[("Qe", ts, g)] = ec
                            plan[("QtrG", ts, g)] = len(Gev); Gev.append(ec)
                    tc += 1; plan[("KVtr", key)] = tc
                    ec += 1; plan[("Ke", key)] = ec
                    plan[("KVtrG", key)] = len(Gev); Gev.append(ec)
                for ev, i in vorder:
                    kind, idx = subs[i]
                    key = ("h", idx) if kind == "halo" else idx
                    if ev == "N":
                        vc += 1; plan[("v1", i)] = vc
                        vc += 1; plan[("vsq", i)] = vc
                        vc += 1; plan[("v2", i)] = vc
                        vc += 1; plan[("v3", i)] = vc
                    else:
                        if kind == "main":
                            vc += 1; plan[("vqe", idx)] = vc
                            vc += 1; plan[("vqr", idx)] = vc
                        vc += 1; plan[("vke", key)] = vc
                        vc += 1; plan[("vkr", key)] = vc

                def ptg(eng, Gidx):
                    if Gidx >= 2:
                        eng.wait_ge(sE.h, Gev[Gidx - 2])

                @blk.sync
                def _(s):
                    if t == 0:
                        for _j in range(4):
                            if poutS[_j].n:
                                s.wait_ge(poutS[_j].h, poutS[_j].n)
                    def xload(i):
                        kind, idx = subs[i]
                        srcx = halo_r[idx] if kind == "halo" else x_r[t * TS + idx]
                        xS[i % 2].inc(s.dma_start(xslot[i % 2], srcx), 16)
                        xtot[i] = xS[i % 2].n
                    xload(0)
                    xload(1)
                    nr = 8 if t == 0 else 4
                    r0_ = 0 if t == 0 else 4 + t * TS
                    sl = 0 if t == 0 else 4
                    csS[0].inc(s.dma_start(cos_sb[:, sl:sl + nr, :],
                                           cos_in[r0_:r0_ + nr].rearrange("n p f -> p n f")), 16)
                    cstot[0] = csS[0].n
                    csS[1].inc(s.dma_start(sin_sb[:, sl:sl + nr, :],
                                           sin_in[r0_:r0_ + nr].rearrange("n p f -> p n f")), 16)
                    cstot[1] = csS[1].n
                    for kc in range(KC):
                        if t == 0:
                            wait_piece(s, ("wkv", kc))
                        slabS[kc].inc(s.dma_start(wkv_slab[kc], c_wkv[kc]), 16)
                        stot[("kv", kc)] = slabS[kc].n
                    for kc in range(KC):
                        if t == 0:
                            wait_piece(s, ("wq", kc))
                        qslabS[kc].inc(s.dma_start(wq_slab[kc], c_wq[kc]), 16)
                        stot[("q", kc)] = qslabS[kc].n
                    for i in range(2, len(subs)):
                        s.wait_ge(sV.h, plan[("v3", i - 2)])
                        xload(i)

                @blk.tensor
                def _(te):
                    for i, (kind, idx) in enumerate(subs):
                        key = ("h", idx) if kind == "halo" else idx
                        nb = plan[("nb", i)]
                        for g in range(nb):
                            ptg(te, plan[("NtrG", i, g)])
                            if g == 0:
                                te.wait_ge(sV.h, plan[("v3" if kind == "halo" else "v2", i)])
                            if kind == "main" and g == 4:
                                te.wait_ge(sV.h, plan[("v3", i)])
                            src_tm = xn_tm if (kind == "main" and g < 4) else xn2_tm
                            ch0 = (g % 4) * 4
                            for ci in range(4):
                                ch = ch0 + ci
                                ins = te.matmul(pT[plan[("NtrG", i, g)] % 2][:, ci * 128:(ci + 1) * 128],
                                                src_tm[:, ch * 128:(ch + 1) * 128],
                                                identb[:, :], start=True, stop=True)
                            sT.inc(ins)
                        if kind == "main":
                            ts = idx
                            te.wait_ge(sE.h, plan[("Ne", i, nb - 1)])
                            if i >= 1:
                                pk, pi = subs[i - 1]
                                te.wait_ge(sV.h, plan[("vke", ("h", pi) if pk == "halo" else pi)])
                            elif t > 0:
                                pass
                            for kc in range(KC):
                                te.wait_ge(qslabS[kc].h, stot[("q", kc)])
                                lhsT = xn2_fm[:, kc, ts * 128:(ts + 1) * 128]
                                for dc in range(4):
                                    ins = te.matmul(Bk[dc][:, :], lhsT,
                                                    wq_slab[kc][:, dc * 512:(dc + 1) * 512],
                                                    start=(kc == 0), stop=(kc == KC - 1))
                            sT.inc(ins)
                            te.wait_ge(sV.h, plan[("vqe", ts)])
                            for kc in range(KC):
                                te.wait_ge(slabS[kc].h, stot[("kv", kc)])
                                lhsT = xn2_fm[:, kc, ts * 128:(ts + 1) * 128]
                                te.matmul(Bk[0][:, :], lhsT, wkv_slab[kc][:, 0:512],
                                          start=(kc == 0), stop=(kc == KC - 1))
                                ins = te.matmul(Bk[1][:, :], lhsT, wkv_slab[kc][:, 512:1024],
                                                start=(kc == 0), stop=(kc == KC - 1))
                            sT.inc(ins)
                            te.wait_ge(sV.h, plan[("vqr", ts)])
                            for g in range(4):
                                ptg(te, plan[("QtrG", ts, g)])
                                for hi in range(4):
                                    h = g * 4 + hi
                                    ins = te.matmul(pT[plan[("QtrG", ts, g)] % 2][:, hi * 128:(hi + 1) * 128],
                                                    qrot[:, h * 128:(h + 1) * 128],
                                                    identb[:, :], start=True, stop=True)
                                sT.inc(ins)
                        else:
                            te.wait_ge(sE.h, plan[("Ne", i, nb - 1)])
                            if i >= 1:
                                te.wait_ge(sV.h, plan[("vke", ("h", idx - 1))])
                            for kc in range(KC):
                                te.wait_ge(slabS[kc].h, stot[("kv", kc)])
                                lhsT = halo_xn2[:, kc, idx * 128:(idx + 1) * 128]
                                te.matmul(Bk[0][:, :], lhsT, wkv_slab[kc][:, 0:512],
                                          start=(kc == 0), stop=(kc == KC - 1))
                                ins = te.matmul(Bk[1][:, :], lhsT, wkv_slab[kc][:, 512:1024],
                                                start=(kc == 0), stop=(kc == KC - 1))
                            sT.inc(ins)
                        te.wait_ge(sV.h, plan[("vkr", key)])
                        ptg(te, plan[("KVtrG", key)])
                        for ci in range(4):
                            ins = te.matmul(pT[plan[("KVtrG", key)] % 2][:, ci * 128:(ci + 1) * 128],
                                            krot[ci * 128:(ci + 1) * 128] if False else
                                            krot[:, ci * 128:(ci + 1) * 128],
                                            identb[:, :], start=True, stop=True)
                        sT.inc(ins)
                    assert sT.n == tc

                @blk.vector
                def _(v):
                    v.wait_ge(csS[0].h, cstot[0])
                    v.wait_ge(csS[1].h, cstot[1])

                    def nchain(i):
                        kind, idx = subs[i]
                        xb = xslot[i % 2]
                        v.wait_ge(xS[i % 2].h, xtot[i])
                        if i >= 1:
                            v.wait_ge(sS.h, plan[("ssq", i - 1)])
                        ve(v, v.reduce_sum(sv(CSUM), xb, axis=AX.X))
                        sV.inc(v.tensor_scalar(sv(CNM), sv(CSUM), -INV_D, None, ALU.mult))
                        v.wait_ge(sS.h, plan[("ssq", i)])
                        sV.inc(v.tensor_tensor(sv(CVAR), sv(CVA), sv(CVB), ALU.add))
                        v.wait_ge(sS.h, plan[("s1", i)])
                        if kind == "main":
                            v.tensor_scalar(sv(CM2), sv(CIVE), sv(CVAR), INV_D,
                                            ALU.mult, ALU.mult)
                            if i >= 1:
                                v.wait_ge(sT.h, plan[("Ntr", i - 1, plan[("nb", i - 1)] - 1)])
                            sV.inc(v.tensor_scalar(xn_tm[:, :], xb, sv(CNM), sv(CR1),
                                                   ALU.add, ALU.mult))
                        else:
                            sV.inc(v.tensor_scalar(sv(CM2), sv(CIVE), sv(CVAR), INV_D,
                                                   ALU.mult, ALU.mult))
                        v.wait_ge(sS.h, plan[("s2", i)])
                        ve(v, v.tensor_tensor(sv(CR2), sv(CR1), sv(CC2), ALU.mult))
                        if i >= 1:
                            v.wait_ge(sT.h, plan[("Ntr", i - 1, plan[("nb", i - 1)] - 1)])
                        sV.inc(v.tensor_scalar(xn2_tm[:, :], xb, sv(CNM), sv(CR2),
                                               ALU.add, ALU.mult))

                    def qkv(i):
                        kind, idx = subs[i]
                        key = ("h", idx) if kind == "halo" else idx
                        if kind == "main":
                            ts = idx
                            v.wait_ge(sT.h, plan[("Qmm", ts)])
                            for dc in range(4):
                                ins = v.tensor_copy(qtm[:, dc * 512:(dc + 1) * 512], Bk[dc][:, :])
                            sV.inc(ins)
                            v.wait_ge(sV.h, sV.n)
                            qt3 = qtm.rearrange("p (h f) -> p h f", h=H)
                            qr3 = qrot.rearrange("p (h f) -> p h f", h=H)
                            c3 = cos_sb[:, 4 + ts, :].unsqueeze(1).broadcast_to([128, 8, 64])
                            s3 = sin_sb[:, 4 + ts, :].unsqueeze(1).broadcast_to([128, 8, 64])
                            rA = ropeA.rearrange("p (h f) -> p h f", h=8)
                            rB = ropeB.rearrange("p (h f) -> p h f", h=8)
                            for hh in range(2):
                                hs_ = slice(hh * 8, (hh + 1) * 8)
                                v.tensor_tensor(rA, qt3[:, hs_, 0:64], c3, ALU.mult)
                                ve(v, v.tensor_tensor(rB, qt3[:, hs_, 64:128], s3, ALU.mult))
                                v.tensor_tensor(qr3[:, hs_, 0:64], rA, rB, ALU.subtract)
                                v.tensor_tensor(rA, qt3[:, hs_, 0:64], s3, ALU.mult)
                                ve(v, v.tensor_tensor(rB, qt3[:, hs_, 64:128], c3, ALU.mult))
                                ins = v.tensor_tensor(qr3[:, hs_, 64:128], rA, rB, ALU.add)
                            sV.inc(ins)
                        slot_ = cur if kind == "main" else prv
                        csl = (4 + idx) if kind == "main" else idx
                        v.wait_ge(sT.h, plan[("KVmm", key)])
                        if kind == "main" and idx >= 1:
                            v.wait_ge(sT.h, plan[("KVtr", idx - 1)])
                        elif kind == "main" and t == 0:
                            v.wait_ge(sT.h, plan[("KVtr", ("h", 3))])
                        elif kind == "halo" and idx >= 1:
                            v.wait_ge(sT.h, plan[("KVtr", ("h", idx - 1))])
                        v.tensor_copy(ktm, Bk[0][:, :])
                        sV.inc(v.tensor_copy(v_ring[:, slot_, idx, :], Bk[1][:, :]))
                        v.wait_ge(sV.h, sV.n)
                        kt = ktm.rearrange("p (h f) -> p h f", h=KH)
                        kr = krot.rearrange("p (h f) -> p h f", h=KH)
                        c4 = cos_sb[:, csl, :].unsqueeze(1).broadcast_to([128, KH, 64])
                        s4 = sin_sb[:, csl, :].unsqueeze(1).broadcast_to([128, KH, 64])
                        mA_ = kmA.rearrange("p (h f) -> p h f", h=KH)
                        mB_ = kmB.rearrange("p (h f) -> p h f", h=KH)
                        v.tensor_tensor(mA_, kt[:, :, 0:64], c4, ALU.mult)
                        ve(v, v.tensor_tensor(mB_, kt[:, :, 64:128], s4, ALU.mult))
                        v.tensor_tensor(kr[:, :, 0:64], mA_, mB_, ALU.subtract)
                        v.tensor_tensor(mA_, kt[:, :, 0:64], s4, ALU.mult)
                        ve(v, v.tensor_tensor(mB_, kt[:, :, 64:128], c4, ALU.mult))
                        sV.inc(v.tensor_tensor(kr[:, :, 64:128], mA_, mB_, ALU.add))

                    for ev, i in vorder:
                        if ev == "N":
                            nchain(i)
                        else:
                            qkv(i)
                    assert sV.n == vc

                @blk.scalar
                def _(sc):
                    if o_guard[0] or o_guard[1]:
                        sc.wait_ge(sO[0].h, o_guard[0])
                        sc.wait_ge(sO[1].h, o_guard[1])

                    def chain1(i):
                        kind, idx = subs[i]
                        sc.wait_ge(sV.h, plan[("v1", i)])
                        se(sc, sc.activation(sqdump[:, :], xslot[i % 2][:, 0:1024],
                                             AF.Square, bias=sv(CNM), accum_out=sv(CVA)))
                        sS.inc(sc.activation(sqdump[:, :], xslot[i % 2][:, 1024:2048],
                                             AF.Square, bias=sv(CNM), accum_out=sv(CVB)))
                        sc.wait_ge(sV.h, plan[("vsq", i)])
                        se(sc, sc.activation(sv(CL1), sv(CVAR), AF.Ln, scale=INV_D,
                                             bias=constv[:, 0:1]))
                        sc.activation(sv(CR1), sv(CL1), AF.Exp, scale=-0.5)
                        sS.inc(sc.activation(sv(CIVE), sv(CL1), AF.Exp, scale=-1.0))

                    def chain2(i):
                        sc.wait_ge(sV.h, plan[("v2", i)])
                        se(sc, sc.activation(sv(CL2), sv(CM2), AF.Ln, bias=constv[:, 1:2]))
                        sS.inc(sc.activation(sv(CC2), sv(CL2), AF.Exp, scale=-0.5))

                    def evacsN(i):
                        kind, idx = subs[i]
                        nb = plan[("nb", i)]
                        for g in range(nb):
                            sc.wait_ge(sT.h, plan[("Ntr", i, g)])
                            ch0 = (g % 4) * 4
                            if kind == "main":
                                dstt = (xn_fm if g < 4 else xn2_fm)[:, ch0:ch0 + 4,
                                                                    idx * 128:(idx + 1) * 128]
                            else:
                                dstt = halo_xn2[:, ch0:ch0 + 4, idx * 128:(idx + 1) * 128]
                            srcp = pT[plan[("NtrG", i, g)] % 2][:, :].rearrange("p (a b) -> p a b", a=4)
                            sE.inc(sc.copy(dstt, srcp))

                    def evacsQK(i):
                        kind, idx = subs[i]
                        key = ("h", idx) if kind == "halo" else idx
                        if kind == "main":
                            for g in range(4):
                                sc.wait_ge(sT.h, plan[("Qtr", idx, g)])
                                dstt = q_fm[:, g * 4:(g + 1) * 4, idx * 128:(idx + 1) * 128]
                                srcp = pT[plan[("QtrG", idx, g)] % 2][:, :].rearrange("p (a b) -> p a b", a=4)
                                sE.inc(sc.copy(dstt, srcp))
                        sc.wait_ge(sT.h, plan[("KVtr", key)])
                        slot_ = cur if kind == "main" else prv
                        dstt = k_ring[:, slot_, :, idx * 128:(idx + 1) * 128]
                        srcp = pT[plan[("KVtrG", key)] % 2][:, :].rearrange("p (a b) -> p a b", a=4)
                        sE.inc(sc.copy(dstt, srcp))

                    for i in range(len(subs)):
                        chain1(i)
                        if i >= 1:
                            evacsN(i - 1)
                        chain2(i)
                        if i >= 1:
                            evacsQK(i - 1)
                    evacsN(len(subs) - 1)
                    evacsQK(len(subs) - 1)
                    assert sS.n == scc and sE.n == ec


            with nc.Block(f"A{t}") as blk:
                t0_, v0, s0 = sT.n, sV.n, sS.n
                mtot = [mS.n]


                @blk.sync
                def _(s):
                    if t == 0:
                        mS.inc(s.dma_start(mT0[:, :, :],
                                           mt0_in.rearrange("n p f -> p n f")), 16)
                        mtot[0] = mS.n

                def t_scores(a, hu):
                    return t0_ + 10 * a + hu + 1

                def t_trb(a, j):
                    return t0_ + 10 * a + 4 + j + 1

                def t_av(a):
                    return t0_ + 10 * (a + 1)

                def v_bias(a, hu):
                    return v0 + 9 * a + 2 * hu + 1

                def v_norm(a, hu):
                    return v0 + 9 * a + 2 * hu + 2

                def v_aev(a):
                    return v0 + 9 * (a + 1)

                def s_exp(a, hu):
                    return s0 + 9 * a + hu + 1

                def s_pev(a, j):
                    return s0 + 9 * a + 4 + j + 1

                units = [(kg, qt) for kg in range(KH) for qt in range(TS)]

                @blk.tensor
                def _(te):
                    for a, (kg, qt) in enumerate(units):
                        par = a % 2
                        N1 = 512 - qt * 128
                        N2 = 128 + qt * 128
                        for hu in range(4):
                            h = kg * 4 + hu
                            u_abs = 4 * a + hu
                            if u_abs >= 2:
                                prev_a, prev_h = divmod(u_abs - 2, 4)
                                te.wait_ge(sS.h, s_exp(prev_a, prev_h))
                            pp = hu % 2
                            SAB = pSAB[pp]
                            lq = q_fm[:, h, qt * 128:(qt + 1) * 128]
                            te.matmul(SAB[:, 512 - N1:512], lq,
                                      k_ring[:, prv, kg, qt * 128:512],
                                      start=True, stop=True)
                            ins = te.matmul(SAB[:, 512:512 + N2], lq,
                                            k_ring[:, cur, kg, 0:N2],
                                            start=True, stop=True)
                            sT.inc(ins)
                        te.wait_ge(sV.h, v_norm(a, 3))
                        if a >= 1:
                            pass
                        for j in range(5):
                            gb = 5 * a + j
                            if gb >= 2:
                                pa, pj = divmod(gb - 2, 5)
                                te.wait_ge(sS.h, s_pev(pa, pj))
                            for hu in range(4):
                                ins = te.matmul(pT[gb % 2][:, hu * 128:(hu + 1) * 128],
                                                probs_n[par][hu][:, j * 128:(j + 1) * 128],
                                                identb[:, :], start=True, stop=True)
                            sT.inc(ins)
                        te.wait_ge(sS.h, s_pev(a, 4))
                        if a >= 2:
                            te.wait_ge(sV.h, v_aev(a - 2))
                        AVb = Bk[4 + par]
                        for j in range(5):
                            wcol = qt + j
                            slot, st = (prv, wcol) if wcol < 4 else (cur, wcol - 4)
                            ins = te.matmul(AVb[:, :],
                                            v_ring[:, slot, st, kg * 128:(kg + 1) * 128],
                                            probsT[par][:, j, :, :].rearrange("p a b -> p (a b)"),
                                            start=(j == 0), stop=(j == 4))
                        sT.inc(ins)

                @blk.vector
                def _(v):
                    if t == 0:
                        v.wait_ge(mS.h, mtot[0])
                    for a, (kg, qt) in enumerate(units):
                        par = a % 2
                        N1 = 512 - qt * 128
                        N2 = 128 + qt * 128
                        for hu in range(4):
                            v.wait_ge(sT.h, t_scores(a, hu))
                            pp = hu % 2
                            SAB = pSAB[pp]
                            strip = SAB[:, 512 - N1:512 + N2]
                            if t == 0:
                                ve(v, v.tensor_tensor(strip, strip, mT0[:, qt, :], ALU.add))
                            else:
                                v.tensor_tensor(SAB[:, 512 - N1:512 - N1 + 128],
                                                SAB[:, 512 - N1:512 - N1 + 128],
                                                mP1[:, :], ALU.add)
                                ve(v, v.tensor_tensor(SAB[:, 512 + N2 - 128:512 + N2],
                                                      SAB[:, 512 + N2 - 128:512 + N2],
                                                      mP2[:, :], ALU.add))
                            ve(v, v.reduce_max(smx(pp, 2), strip, axis=AX.X, negate=True))
                            sV.inc(v.tensor_scalar(smx(pp, 3), smx(pp, 2), SCALE,
                                                   None, ALU.mult))
                            v.wait_ge(sS.h, s_exp(a, hu))
                            ve(v, v.reciprocal(smx(pp, 7), smx(pp, 6)))
                            sV.inc(v.tensor_scalar(probs_n[par][hu][:, :],
                                                   probs_n[par][hu][:, :],
                                                   smx(pp, 7), None, ALU.mult))
                        v.wait_ge(sT.h, t_av(a))
                        dstt = attnout[:, kg * 4:(kg + 1) * 4, qt * 128:(qt + 1) * 128]
                        sV.inc(v.tensor_copy(dstt,
                                             Bk[4 + par][:, :].rearrange("p (a b) -> p a b", a=4)))

                @blk.scalar
                def _(sc):
                    for a, (kg, qt) in enumerate(units):
                        par = a % 2
                        N1 = 512 - qt * 128
                        N2 = 128 + qt * 128
                        for hu in range(4):
                            sc.wait_ge(sV.h, v_bias(a, hu))
                            pp = hu % 2
                            SAB = pSAB[pp]
                            sS.inc(sc.activation(probs_n[par][hu][:, :],
                                                 SAB[:, 512 - N1:512 + N2], AF.Exp,
                                                 bias=smx(pp, 3), scale=SCALE,
                                                 accum_out=smx(pp, 6)))
                        for j in range(5):
                            sc.wait_ge(sT.h, t_trb(a, j))
                            sS.inc(sc.copy(probsT[par][:, j, :, :],
                                                  pT[(5 * a + j) % 2][:, :].rearrange(
                                                      "p (a b) -> p a b", a=4)))

                assert sT.n == t0_ + 160 and sV.n == v0 + 144 and sS.n == s0 + 144

            with nc.Block(f"O{t}") as blk:
                t0_, v0 = sT.n, sV.n
                wotot = {}


                @blk.sync
                def _(s):
                    for oc in range(KC):
                        if oc >= 2:
                            s.wait_ge(sT.h, t0_ + oc - 1)
                        if t == 0:
                            wait_piece(s, ("wo", oc))
                        woS[oc % 2].inc(s.dma_start(woblk[:, oc % 2, :, :], c_wo[oc]), 16)
                        wotot[oc] = woS[oc % 2].n

                @blk.tensor
                def _(te):
                    for oc in range(KC):
                        te.wait_ge(woS[oc % 2].h, wotot[oc])
                        if oc >= 2:
                            te.wait_ge(sV.h, v0 + oc - 1)
                        for kc in range(KC):
                            ins = te.matmul(Bk[oc % 2][:, :], woblk[:, oc % 2, kc, :],
                                            attnout[:, kc, :],
                                            start=(kc == 0), stop=(kc == KC - 1))
                        sT.inc(ins)

                @blk.vector
                def _(v):
                    for oc in range(KC):
                        v.wait_ge(sT.h, t0_ + oc + 1)
                        sV.inc(v.tensor_copy(s_fm[:, oc, :], Bk[oc % 2][:, :]))

            with nc.Block(f"F{t}") as blk:
                t0_, v0, s0 = sT.n, sV.n, sS.n
                wgtot, wutot, wdtot = {}, {}, {}

                tg = lambda oc: t0_ + 2 * oc + 1
                tu = lambda oc: t0_ + 2 * oc + 2
                sg_ = lambda oc: s0 + oc + 1
                vu = lambda oc: v0 + 2 * oc + 1
                vf = lambda oc: v0 + 2 * oc + 2
                td = lambda ocd, q: t0_ + 128 + 4 * ocd + q + 1
                sd_ = lambda ocd: s0 + 64 + ocd + 1
                vsg = lambda ocd: v0 + 128 + ocd + 1

                @blk.sync
                def _(s):
                    for oc in range(OCI):
                        if oc >= 2:
                            s.wait_ge(sT.h, tg(oc - 2))
                        if t == 0:
                            wait_piece(s, ("wg", oc))
                        wgS[oc % 2].inc(s.dma_start(wgblk[:, oc % 2, :, :], c_wg[oc]), 16)
                        wgtot[oc] = wgS[oc % 2].n
                        if oc >= 1:
                            s.wait_ge(sT.h, tu(oc - 1))
                        if t == 0:
                            wait_piece(s, ("wu", oc))
                        wuS.inc(s.dma_start(wublk[:, :, :], c_wu[oc]), 16)
                        wutot[oc] = wuS.n
                    for ocd in range(KC):
                        for q in range(4):
                            gq = ocd * 4 + q
                            if gq >= 3:
                                pocd, pq = divmod(gq - 3, 4)
                                s.wait_ge(sT.h, td(pocd, pq))
                            if t == 0:
                                wait_piece(s, ("wd", ocd, q))
                            wdS[gq % 3].inc(s.dma_start(wdblk[:, (gq % 3) * 16:(gq % 3 + 1) * 16, :],
                                                        c_wd[ocd][:, q * 16 * 128:(q + 1) * 16 * 128]), 16)
                            wdtot[(ocd, q)] = wdS[gq % 3].n

                @blk.tensor
                def _(te):
                    for oc in range(OCI):
                        te.wait_ge(wgS[oc % 2].h, wgtot[oc])
                        if oc >= 2:
                            te.wait_ge(sS.h, sg_(oc - 2))
                        for kc in range(KC):
                            ins = te.matmul(Bk[oc % 2][:, :], wgblk[:, oc % 2, kc, :],
                                            xn_fm[:, kc, :],
                                            start=(kc == 0), stop=(kc == KC - 1))
                        sT.inc(ins)
                        te.wait_ge(wuS.h, wutot[oc])
                        if oc >= 2:
                            te.wait_ge(sV.h, vu(oc - 2))
                        for kc in range(KC):
                            ins = te.matmul(Bk[2 + oc % 2][:, :], wublk[:, kc, :],
                                            xn_fm[:, kc, :],
                                            start=(kc == 0), stop=(kc == KC - 1))
                        sT.inc(ins)
                    for ocd in range(KC):
                        if ocd >= 2:
                            te.wait_ge(sS.h, sd_(ocd - 2))
                        for q in range(4):
                            gq = ocd * 4 + q
                            te.wait_ge(wdS[gq % 3].h, wdtot[(ocd, q)])
                            if ocd == 0:
                                te.wait_ge(sV.h, vf((q + 1) * 16 - 1))
                            for kc in range(q * 16, (q + 1) * 16):
                                ins = te.matmul(Bk[4 + ocd % 2][:, :],
                                                wdblk[:, (gq % 3) * 16 + (kc - q * 16), :],
                                                fuse[:, kc, :],
                                                start=(kc == 0), stop=(kc == OCI - 1))
                            sT.inc(ins)

                @blk.scalar
                def _(sc):
                    for oc in range(OCI):
                        sc.wait_ge(sT.h, tg(oc))
                        if oc >= 2:
                            sc.wait_ge(sV.h, vf(oc - 2))
                        sS.inc(sc.activation(gate_oc[:, oc % 2, :], Bk[oc % 2][:, :],
                                             AF.Silu, scale=sgate_t[:, oc:oc + 1]))
                    for ocd in range(KC):
                        sc.wait_ge(sT.h, td(ocd, 3))
                        if ocd >= 2:
                            sc.wait_ge(sV.h, vsg(ocd - 2))
                        sS.inc(sc.activation(g_oc[:, ocd % 2, :], Bk[4 + ocd % 2][:, :],
                                             AF.Silu, scale=sdown_t[:, ocd:ocd + 1]))

                @blk.vector
                def _(v):
                    for oc in range(OCI):
                        v.wait_ge(sT.h, tu(oc))
                        sV.inc(v.tensor_scalar(up_oc[:, oc % 2, :], Bk[2 + oc % 2][:, :],
                                               sup_t[:, oc:oc + 1], None, ALU.mult))
                        v.wait_ge(sV.h, sV.n)
                        v.wait_ge(sS.h, sg_(oc))
                        sV.inc(v.tensor_tensor(fuse[:, oc, :], gate_oc[:, oc % 2, :],
                                               up_oc[:, oc % 2, :], ALU.mult))
                    for ocd in range(KC):
                        v.wait_ge(sS.h, sd_(ocd))
                        sV.inc(v.tensor_tensor(sg_fm[:, ocd, :], s_fm[:, ocd, :],
                                               g_oc[:, ocd % 2, :], ALU.mult))

                assert sT.n == t0_ + 192 and sS.n == s0 + 80 and sV.n == v0 + 144

            with nc.Block(f"E{t}") as blk:
                t0_, v0 = sT.n, sV.n
                o0 = (sO[0].n, sO[1].n)
                xtot = {}

                @blk.scalar
                def _(sc):
                    for ts in range(TS):
                        if ts >= 2:
                            sc.wait_ge(sV.h, v0 + 4 * (ts - 1))
                        xS[ts % 2].inc(sc.dma_start(xslot[ts % 2], x_r[t * TS + ts]), 16)
                        xtot[ts] = xS[ts % 2].n

                @blk.tensor
                def _(te):
                    for ts in range(TS):
                        for g in range(4):
                            G = 4 * ts + g
                            if G >= 2:
                                te.wait_ge(sV.h, v0 + G - 1)
                            for i in range(4):
                                oc = g * 4 + i
                                ins = te.matmul(pT[G % 2][:, i * 128:(i + 1) * 128],
                                                sg_fm[:, oc, ts * 128:(ts + 1) * 128],
                                                identb[:, :], start=True, stop=True)
                            sT.inc(ins)

                @blk.vector
                def _(v):
                    for ts in range(TS):
                        v.wait_ge(xS[ts % 2].h, xtot[ts])
                        if ts >= 2:
                            v.wait_ge(sO[ts % 2].h, o0[ts % 2] + 16)
                        for g in range(4):
                            G = 4 * ts + g
                            v.wait_ge(sT.h, t0_ + G + 1)
                            sV.inc(v.tensor_tensor(out_stage[ts % 2][:, g * 512:(g + 1) * 512],
                                                   pT[G % 2][:, :],
                                                   xslot[ts % 2][:, g * 512:(g + 1) * 512],
                                                   ALU.add))

                @blk.sync
                def _(s):
                    for ts in range(TS):
                        s.wait_ge(sV.h, v0 + 4 * (ts + 1))
                        sO[ts % 2].inc(s.dma_start(out_r[t * TS + ts], out_stage[ts % 2]), 16)

        with nc.Block("fin") as blk:
            fin = (sO[0].n, sO[1].n)

            @blk.sync
            def _(s):
                s.wait_ge(sO[0].h, fin[0])
                s.wait_ge(sO[1].h, fin[1])

    return nc


_PROG_CACHE = {}


def _get_program():
    if "nc" not in _PROG_CACHE:
        _PROG_CACHE["nc"] = build_program()
    return _PROG_CACHE["nc"]


def _rope_perm(nheads):
    perm = []
    for h in range(nheads):
        base = h * 128
        perm += [base + 2 * i for i in range(64)]
        perm += [base + 2 * i + 1 for i in range(64)]
    return np.array(perm, dtype=np.int64)


def kernel(L0_attn_x, L0_attn_ln_w, L0_attn_wq, L0_attn_wk, L0_attn_wv, L0_attn_wo,
           L0_ffn_w_gate, L0_ffn_w_up, L0_ffn_w_down, L0_ffn_s_gate, L0_ffn_s_up,
           L0_ffn_s_down):
    x = np.ascontiguousarray(np.asarray(L0_attn_x, dtype=np.float32))
    wq = np.ascontiguousarray(np.asarray(L0_attn_wq, dtype=np.float32)[:, _rope_perm(H)])
    wk = np.ascontiguousarray(np.asarray(L0_attn_wk, dtype=np.float32)[:, _rope_perm(KH)])
    wv = np.ascontiguousarray(np.asarray(L0_attn_wv, dtype=np.float32))
    wo = np.ascontiguousarray(np.asarray(L0_attn_wo, dtype=np.float32))
    wg = np.ascontiguousarray(np.asarray(L0_ffn_w_gate, dtype=np.int8))
    wu = np.ascontiguousarray(np.asarray(L0_ffn_w_up, dtype=np.int8))
    wd = np.ascontiguousarray(np.asarray(L0_ffn_w_down, dtype=np.int8))
    sgate = np.ascontiguousarray(np.asarray(L0_ffn_s_gate, dtype=np.float32).reshape(OCI, 128).T)
    sup = np.ascontiguousarray(np.asarray(L0_ffn_s_up, dtype=np.float32).reshape(OCI, 128).T)
    sdown = np.ascontiguousarray(np.asarray(L0_ffn_s_down, dtype=np.float32).reshape(KC, 128).T)
    lnw = np.ascontiguousarray(np.asarray(L0_attn_ln_w, dtype=np.float32).reshape(KC, 128).T)

    qi = np.arange(128)[:, None]
    cc = np.arange(128)[None, :]
    mP1 = np.where(qi <= cc, 0.0, NEG).astype(np.float32)
    mP2 = np.where(cc <= qi, 0.0, NEG).astype(np.float32)
    identm = np.eye(128, dtype=ml_dtypes.bfloat16)

    inv_freq = (1.0 / 10000.0 ** (np.arange(0, HD, 2, dtype=np.float32) / HD)).astype(np.float32)

    def trig_tables(seq_off):
        pos = np.concatenate([np.arange(seq_off - W, seq_off), np.arange(seq_off, seq_off + SC)])
        pos = np.maximum(pos, 0).astype(np.float32)
        ang = pos[:, None] * inv_freq[None, :]
        return (np.cos(ang).astype(ml_dtypes.bfloat16).reshape(-1, 128, 64),
                np.sin(ang).astype(ml_dtypes.bfloat16).reshape(-1, 128, 64))

    def t0_mask(first_half):
        m = np.empty((4, 128, 640), dtype=np.float32)
        for qt in range(4):
            cp = np.arange(640)[None, :]
            valid = (qi <= cp) & (cp <= qi + 512)
            if first_half:
                valid &= (qt * 128 + cp) >= 512
            m[qt] = np.where(valid, 0.0, NEG)
        return m

    nc = _get_program()
    in_maps = []
    for c in range(NCORE):
        b, half = c // 2, c % 2
        off = half * SC
        xc = np.ascontiguousarray(x[b, off:off + SC])
        haloc = (np.zeros((W, D), dtype=np.float32) if half == 0
                 else np.ascontiguousarray(x[b, off - W:off]))
        cosb, sinb = trig_tables(off)
        in_maps.append({
            "x": xc, "halo": haloc, "wq": wq, "wk": wk, "wv": wv, "wo": wo,
            "wg": wg, "wu": wu, "wd": wd, "sgate": sgate, "sup": sup,
            "sdown": sdown, "lnw": lnw, "cosb": cosb, "sinb": sinb,
            "mP1": mP1, "mP2": mP2, "mT0": t0_mask(half == 0), "identb": identm,
        })

    res = run_bass_kernel_spmd(nc, in_maps, core_ids=list(range(NCORE)))
    out = np.empty((B, S, D), dtype=np.float32)
    for c in range(NCORE):
        b, half = c // 2, c % 2
        out[b, half * SC:(half + 1) * SC] = res.results[c]["out"]
    return out


# version 21
# speedup vs baseline: 1.0713x; 1.0062x over previous
import math
import numpy as np
import ml_dtypes
import concourse.bass as bass
import concourse.mybir as mybir
from concourse.bass_utils import run_bass_kernel_spmd

F32 = mybir.dt.float32
BF16 = mybir.dt.bfloat16
I8 = mybir.dt.int8
AF = mybir.ActivationFunctionType
ALU = mybir.AluOpType
AX = mybir.AxisListType

B, S, D = 4, 4096, 2048
NCORE = 8
SC = 2048
TILE = 512
NT = SC // TILE
TS = TILE // 128
H, KH, HD = 16, 4, 128
W = 512
INTER = 8192
KC = D // 128
OCI = INTER // 128
SCALE = 1.0 / math.sqrt(HD)
NEG = -1.0e30
INV_D = 1.0 / D
EPS_LN = 1.0e-5
EPS_RMS = 1.0e-6


class Sem:
    def __init__(self, nc, name):
        self._cm = nc.semaphore(name)
        self.h = self._cm.__enter__()
        self.n = 0

    def inc(self, ins, k=1):
        ins.then_inc(self.h, k)
        self.n += k
        return self.n


def build_program():
    nc = bass.Bass("TRN2", target_bir_lowering=False)

    x_in = nc.dram_tensor("x", [SC, D], F32, kind="ExternalInput").ap()
    halo_in = nc.dram_tensor("halo", [W, D], F32, kind="ExternalInput").ap()
    wq_in = nc.dram_tensor("wq", [D, D], F32, kind="ExternalInput").ap()
    wk_in = nc.dram_tensor("wk", [D, KH * HD], F32, kind="ExternalInput").ap()
    wv_in = nc.dram_tensor("wv", [D, KH * HD], F32, kind="ExternalInput").ap()
    wo_in = nc.dram_tensor("wo", [D, D], F32, kind="ExternalInput").ap()
    wg_in = nc.dram_tensor("wg", [D, INTER], I8, kind="ExternalInput").ap()
    wu_in = nc.dram_tensor("wu", [D, INTER], I8, kind="ExternalInput").ap()
    wd_in = nc.dram_tensor("wd", [INTER, D], I8, kind="ExternalInput").ap()
    sgate_in = nc.dram_tensor("sgate", [128, OCI], F32, kind="ExternalInput").ap()
    sup_in = nc.dram_tensor("sup", [128, OCI], F32, kind="ExternalInput").ap()
    sdown_in = nc.dram_tensor("sdown", [128, KC], F32, kind="ExternalInput").ap()
    lnw_in = nc.dram_tensor("lnw", [128, KC], F32, kind="ExternalInput").ap()
    ncs = (SC + W) // 128
    cos_in = nc.dram_tensor("cosb", [ncs, 128, 64], BF16, kind="ExternalInput").ap()
    sin_in = nc.dram_tensor("sinb", [ncs, 128, 64], BF16, kind="ExternalInput").ap()
    mp1_in = nc.dram_tensor("mP1", [128, 128], F32, kind="ExternalInput").ap()
    mp2_in = nc.dram_tensor("mP2", [128, 128], F32, kind="ExternalInput").ap()
    mt0_in = nc.dram_tensor("mT0", [4, 128, 640], F32, kind="ExternalInput").ap()
    id_in = nc.dram_tensor("identb", [128, 128], BF16, kind="ExternalInput").ap()
    out_hbm = nc.dram_tensor("out", [SC, D], F32, kind="ExternalOutput").ap()

    c_wq = nc.dram_tensor("c_wq", [KC, 128, D], BF16).ap()
    c_wkv = nc.dram_tensor("c_wkv", [KC, 128, 1024], BF16).ap()
    c_wo = nc.dram_tensor("c_wo", [KC, 128, KC * 128], BF16).ap()
    c_wg = nc.dram_tensor("c_wg", [OCI, 128, KC * 128], BF16).ap()
    c_wu = nc.dram_tensor("c_wu", [OCI, 128, KC * 128], BF16).ap()
    c_wd = nc.dram_tensor("c_wd", [KC, 128, OCI * 128], BF16).ap()

    from contextlib import ExitStack
    with ExitStack() as _stk:
        def _sb(*args):
            return _stk.enter_context(nc.sbuf_tensor(*args))

        def _ps(*args):
            return _stk.enter_context(nc.psum_tensor(*args))

        xn_fm = _sb("xn_fm", [128, KC, TILE], BF16)
        xn2_fm = _sb("xn2_fm", [128, KC, TILE], BF16)
        q_fm = _sb("q_fm", [128, H, 128 * TS], BF16)
        k_ring = _sb("k_ring", [128, 2, KH, W], BF16)
        v_ring = _sb("v_ring", [128, 2, TS, KH * HD], BF16)
        s_fm = _sb("s_fm", [128, KC, TILE], BF16)
        bigreg = _sb("bigreg", [128, 65536], I8)
        scratch = _sb("scratch", [128, 16384], I8)
        wgblk = _sb("wgblk", [128, 2, KC, 128], BF16)
        wublk = _sb("wublk", [128, KC, 128], BF16)
        wdblk = _sb("wdblk", [128, 48, 128], BF16)
        woblk = _sb("woblk", [128, 2, KC, 128], BF16)
        gate_oc = _sb("gate_oc", [128, 2, TILE], BF16)
        up_oc = _sb("up_oc", [128, 2, TILE], BF16)
        g_oc = _sb("g_oc", [128, 2, TILE], BF16)
        cos_sb = _sb("cos_sb", [128, 8, 64], BF16)
        sin_sb = _sb("sin_sb", [128, 8, 64], BF16)
        kvm = _sb("kvm", [128, 2, 256], F32)
        mP1 = _sb("mP1s", [128, 128], F32)
        mP2 = _sb("mP2s", [128, 128], F32)
        identb = _sb("idents", [128, 128], BF16)
        sgate_t = _sb("sgate_t", [128, OCI], F32)
        sup_t = _sb("sup_t", [128, OCI], F32)
        sdown_t = _sb("sdown_t", [128, KC], F32)
        lnw_t = _sb("lnw_t", [128, KC], F32)
        constv = _sb("constv", [128, 2], F32)
        statv = _sb("statv", [128, 16], F32)
        smaxv = _sb("smaxv", [128, 2, 8], F32)
        pT0 = _ps("pT0", [128, 512], F32)
        pT1 = _ps("pT1", [128, 512], F32)
        PB01 = _ps("PB01", [128, 1024], F32)
        PB23 = _ps("PB23", [128, 1024], F32)
        PB45 = _ps("PB45", [128, 1024], F32)
        PB4, PB5 = PB45[:, 0:512], PB45[:, 512:1024]
        PB0, PB1 = PB01[:, 0:512], PB01[:, 512:1024]
        PB2, PB3 = PB23[:, 0:512], PB23[:, 512:1024]
        pSAB = [PB01, PB23]
        pT = [pT0, pT1]
        Bk = [PB0, PB1, PB2, PB3, PB4, PB5]
        sW = Sem(nc, "sW")
        sX = Sem(nc, "sX")
        sP = Sem(nc, "sP")
        sO = [Sem(nc, "sO0"), Sem(nc, "sO1")]
        sT = Sem(nc, "sT")
        sV = Sem(nc, "sV")
        sS = Sem(nc, "sS")
        sE = Sem(nc, "sE")
        sZV = Sem(nc, "sZV")
        sZS = Sem(nc, "sZS")
        slabS = [Sem(nc, f"sl{i}") for i in range(KC)]
        qslabS = [Sem(nc, f"qsl{i}") for i in range(KC)]
        xS = [Sem(nc, "xs0"), Sem(nc, "xs1")]
        csS = [Sem(nc, "cs0"), Sem(nc, "cs1")]
        woS = [Sem(nc, "wo0"), Sem(nc, "wo1")]
        wgS = [Sem(nc, "wgS0"), Sem(nc, "wgS1")]
        wuS = Sem(nc, "wuS")
        wdS = [Sem(nc, f"wd{i}") for i in range(3)]
        mS = Sem(nc, "mS")
        sZP = Sem(nc, "sZP")
        pinS = [Sem(nc, f"pi{i}") for i in range(8)]
        poutS = [Sem(nc, f"po{i}") for i in range(8)]

        def ve(v, ins):
            sZV.inc(ins)
            v.wait_ge(sZV.h, sZV.n)

        def se(sc, ins):
            sZS.inc(ins)
            sc.wait_ge(sZS.h, sZS.n)

        def carve(buf, off_b, dt, shape):
            esz = {BF16: 2, F32: 4, I8: 1}[dt]
            nbytes = int(np.prod(shape[1:])) * esz
            ap = buf[:, off_b:off_b + nbytes].bitcast(dt)
            if len(shape) == 3:
                ap = ap.rearrange("p (a b) -> p a b", a=shape[1])
            elif len(shape) == 4:
                ap = ap.rearrange("p (a b c) -> p a b c", a=shape[1], b=shape[2])
            assert tuple(ap.shape) == tuple(shape), (ap.shape, shape)
            return ap

        with nc.Block("consts") as blk:
            @blk.vector
            def _(v):
                v.memset(constv[:, 0:1], EPS_LN)
                v.memset(constv[:, 1:2], EPS_RMS)

            @blk.scalar
            def _(sc):
                for dst, src in ((mP1[:, :], mp1_in), (mP2[:, :], mp2_in),
                                 (identb[:, :], id_in), (sgate_t[:, :], sgate_in),
                                 (sup_t[:, :], sup_in), (sdown_t[:, :], sdown_in),
                                 (lnw_t[:, :], lnw_in)):
                    mS.inc(sc.dma_start(dst, src), 16)
                sc.wait_ge(mS.h, mS.n)

        wq_r = wq_in.rearrange("(k p) n -> k p n", p=128)
        wk_r = wk_in.rearrange("(k p) n -> k p n", p=128)
        wv_r = wv_in.rearrange("(k p) n -> k p n", p=128)
        wo_r = wo_in.rearrange("(k p) n -> p k n", p=128)
        wg_r = wg_in.rearrange("(k p) n -> p k n", p=128)
        wu_r = wu_in.rearrange("(k p) n -> p k n", p=128)
        wd_r = wd_in.rearrange("(k p) n -> p k n", p=128)
        pbI = [carve(bigreg, h * 16384, BF16, [128, KC, 512]) for h in range(2)]
        pbI2 = [carve(bigreg, h * 16384, BF16, [128, 8192]) for h in range(2)]
        pbIf = [carve(bigreg, h * 16384, F32, [128, 4096]) for h in range(2)]
        pbIw = [carve(bigreg, h * 16384, F32, [128, KC, 256]) for h in range(2)]
        pbP = [carve(bigreg, 32768 + h * 16384, BF16, [128, 8192]) for h in range(2)]
        pbPw = [carve(bigreg, 32768 + h * 16384, BF16, [128, 2, KC, 128]) for h in range(2)]

        stages = []
        for kc in range(KC):
            stages.append(dict(
                kind="qkv",
                src=lambda b, kc=kc: [(pbIf[b][:, 0:2048], wq_r[kc]),
                                      (pbIf[b][:, 2048:2560], wk_r[kc]),
                                      (pbIf[b][:, 2560:3072], wv_r[kc])],
                scale=lnw_t[:, kc:kc + 1],
                outs=lambda b, kc=kc: [(c_wq[kc], pbP[b][:, 0:2048]),
                                       (c_wkv[kc], pbP[b][:, 2048:3072])],
                keys=[("wq", kc), ("wkv", kc)]))
        for cg in range(8):
            stages.append(dict(
                kind="wof",
                src=lambda b, cg=cg: [(pbIw[b][:, :, :],
                                       wo_r[:, :, cg * 256:(cg + 1) * 256])],
                outs=lambda b, cg=cg: [(c_wo[cg * 2 + ol],
                                        pbPw[b][:, ol, :, :].rearrange("p a b -> p (a b)"))
                                       for ol in range(2)],
                keys=[("wo", cg * 2 + ol) for ol in range(2)]))
        for cg in range(16):
            for nm, srcr, dstc in (("wg", wg_r, c_wg), ("wu", wu_r, c_wu)):
                stages.append(dict(
                    kind="perm",
                    src=lambda b, cg=cg, srcr=srcr: [(pbI[b][:, :, :],
                                                      srcr[:, :, cg * 512:(cg + 1) * 512])],
                    outs=lambda b, cg=cg, dstc=dstc: [(dstc[cg * 4 + ol],
                                                       pbP[b][:, ol * 2048:(ol + 1) * 2048])
                                                      for ol in range(4)],
                    keys=[(nm, cg * 4 + ol) for ol in range(4)]))
        for kh in range(4):
            for cg in range(4):
                stages.append(dict(
                    kind="perm",
                    src=lambda b, kh=kh, cg=cg: [(pbI[b][:, :, :],
                                                  wd_r[:, kh * 16:(kh + 1) * 16,
                                                       cg * 512:(cg + 1) * 512])],
                    outs=lambda b, kh=kh, cg=cg: [
                        (c_wd[cg * 4 + ol][:, kh * 16 * 128:(kh + 1) * 16 * 128],
                         pbP[b][:, ol * 2048:(ol + 1) * 2048]) for ol in range(4)],
                    keys=[("wd", cg * 4 + ol, kh) for ol in range(4)]))
        NSTAGE = len(stages)
        piece_ready = {}
        out_totals = {}
        in_totals = {}
        _psim = [0, 0]
        for _s, _st in enumerate(stages):
            _psim[_s % 2] += 16 * len(_st["outs"](0))
            out_totals[_s] = _psim[_s % 2]

        with nc.Block("prepass") as blk:
            @blk.gpsimd
            def _(g):
                for s, st in enumerate(stages):
                    bb = s % 2
                    if s >= 2:
                        g.wait_ge(sZP.h, s - 1)
                    inS = pinS[s % 2]
                    for dst, srcap in st["src"](bb):
                        inS.inc(g.dma_start(dst, srcap), 16)
                    in_totals[s] = inS.n

            @blk.vector
            def _(v):
                for s, st in enumerate(stages):
                    bb = s % 2
                    v.wait_ge(pinS[s % 2].h, in_totals[s])
                    if s >= 2:
                        v.wait_ge(poutS[s % 2].h, out_totals[s - 2])
                    if st["kind"] == "qkv":
                        sZP.inc(v.tensor_scalar(pbP[bb][:, 0:3072], pbIf[bb][:, 0:3072],
                                                st["scale"], None, ALU.mult))
                    elif st["kind"] == "wof":
                        for ol in range(2):
                            ins = v.tensor_copy(pbPw[bb][:, ol, :, :],
                                                pbIw[bb][:, :, ol * 128:(ol + 1) * 128])
                        sZP.inc(ins)
                    else:
                        for ol in range(4):
                            ins = v.tensor_copy(
                                pbP[bb][:, ol * 2048:(ol + 1) * 2048].rearrange(
                                    "p (a b) -> p a b", a=KC),
                                pbI[bb][:, :, ol * 128:(ol + 1) * 128])
                        sZP.inc(ins)

            @blk.sync
            def _(s_eng):
                for s, st in enumerate(stages):
                    bb = s % 2
                    ouS = poutS[s % 2]
                    s_eng.wait_ge(sZP.h, s + 1)
                    for dst, srcv in st["outs"](bb):
                        ouS.inc(s_eng.dma_start(dst, srcv), 16)
                    assert ouS.n == out_totals[s], (s, ouS.n, out_totals[s])
                    for key in st["keys"]:
                        piece_ready[key] = (ouS, ouS.n)

        def wait_piece(eng, key):
            sem, val = piece_ready[key]
            eng.wait_ge(sem.h, val)

        x_r = x_in.rearrange("(n p) d -> n p d", p=128)
        halo_r = halo_in.rearrange("(n p) d -> n p d", p=128)
        out_r = out_hbm.rearrange("(n p) d -> n p d", p=128)

        xslot = [carve(scratch, 0, F32, [128, 2048]), carve(scratch, 8192, F32, [128, 2048])]
        sfm_flat = s_fm[:, :, :].rearrange("p a b -> p (a b)")
        xn_tm = sfm_flat[:, 0:2048]
        xn2_tm = sfm_flat[:, 2048:4096]
        sqdump = PB45
        qtm = sfm_flat[:, 4096:6144]
        qrot = sfm_flat[:, 6144:8192]
        ropeA = gate_oc[:, :, :].rearrange("p a b -> p (a b)").bitcast(F32)
        ropeB = up_oc[:, :, :].rearrange("p a b -> p (a b)").bitcast(F32)
        ktm = g_oc[:, 0, :]
        krot = g_oc[:, 1, :]
        kmA = kvm[:, 0, :]
        kmB = kvm[:, 1, :]

        wq_slab = [carve(bigreg, i * 4096, BF16, [128, 2048]) for i in range(KC)]
        _kvhost = ([wdblk[:, :, :].rearrange("p a b -> p (a b)")[:, i * 1024:(i + 1) * 1024]
                    for i in range(6)] +
                   [woblk[:, :, :, :].rearrange("p a b c -> p (a b c)")[:, i * 1024:(i + 1) * 1024]
                    for i in range(4)] +
                   [wgblk[:, :, :, :].rearrange("p a b c -> p (a b c)")[:, i * 1024:(i + 1) * 1024]
                    for i in range(4)] +
                   [wublk[:, :, :].rearrange("p a b -> p (a b)")[:, i * 1024:(i + 1) * 1024]
                    for i in range(2)])
        wkv_slab = _kvhost
        attnout = carve(bigreg, 0, BF16, [128, KC, TILE])
        probs_n = [[carve(bigreg, 16384 + (p * 4 + h) * 1280, BF16, [128, 640])
                    for h in range(4)] for p in range(2)]
        probsT = [carve(bigreg, 26624 + p * 5120, BF16, [128, 5, 4, 128]) for p in range(2)]
        mT0 = carve(bigreg, 36864, F32, [128, 4, 640])
        fuse = carve(bigreg, 0, BF16, [128, OCI, TILE])
        sg_fm = xn_fm
        out_stage = [
            xn2_fm[:, 0:8, :].bitcast(F32).rearrange("p a b -> p (a b)"),
            xn2_fm[:, 8:16, :].bitcast(F32).rearrange("p a b -> p (a b)"),
        ]
        halo_xn2 = q_fm[:, :, :]

        CSUM, CNM, CVAR, CVA, CVB, CL1, CR1, CIVE, CM2, CL2, CC2, CR2 = range(12)

        def sv(c):
            return statv[:, c:c + 1]

        def smx(par, c):
            return smaxv[:, par, c:c + 1]

        for t in range(NT):
            cur = t % 2
            prv = (t - 1) % 2

            subs = ([("halo", hs) for hs in range(4)] if t == 0 else []) + \
                   [("main", ts) for ts in range(TS)]
            with nc.Block(f"NQK{t}") as blk:
                v0, s0, t0_, e0 = sV.n, sS.n, sT.n, sE.n
                o_guard = (sO[0].n, sO[1].n)
                xtot, stot, cstot = {}, {}, {}
                plan = {}
                Gev = []
                vc, scc, tc, ec = v0, s0, t0_, e0
                nsub = len(subs)
                vorder = []
                for i in range(nsub):
                    vorder.append(("N", i))
                    if i >= 1:
                        vorder.append(("QKV", i - 1))
                vorder.append(("QKV", nsub - 1))
                for i, (kind, idx) in enumerate(subs):
                    key = ("h", idx) if kind == "halo" else idx
                    scc += 1; plan[("ssq", i)] = scc
                    scc += 1; plan[("s1", i)] = scc
                    scc += 1; plan[("s2", i)] = scc
                    nb = 4 if kind == "halo" else 8
                    plan[("nb", i)] = nb
                    for g in range(nb):
                        tc += 1; plan[("Ntr", i, g)] = tc
                        ec += 1; plan[("Ne", i, g)] = ec
                        plan[("NtrG", i, g)] = len(Gev); Gev.append(ec)
                    if kind == "main":
                        ts = idx
                        tc += 1; plan[("Qmm", ts)] = tc
                    tc += 1; plan[("KVmm", key)] = tc
                    if kind == "main":
                        for g in range(4):
                            tc += 1; plan[("Qtr", ts, g)] = tc
                            ec += 1; plan[("Qe", ts, g)] = ec
                            plan[("QtrG", ts, g)] = len(Gev); Gev.append(ec)
                    tc += 1; plan[("KVtr", key)] = tc
                    ec += 1; plan[("Ke", key)] = ec
                    plan[("KVtrG", key)] = len(Gev); Gev.append(ec)
                for ev, i in vorder:
                    kind, idx = subs[i]
                    key = ("h", idx) if kind == "halo" else idx
                    if ev == "N":
                        vc += 1; plan[("v1", i)] = vc
                        vc += 1; plan[("vsq", i)] = vc
                        vc += 1; plan[("v2", i)] = vc
                        vc += 1; plan[("v3", i)] = vc
                    else:
                        if kind == "main":
                            vc += 1; plan[("vqe", idx)] = vc
                            vc += 1; plan[("vqr", idx)] = vc
                        vc += 1; plan[("vke", key)] = vc
                        vc += 1; plan[("vkr", key)] = vc

                def ptg(eng, Gidx):
                    if Gidx >= 2:
                        eng.wait_ge(sE.h, Gev[Gidx - 2])

                @blk.sync
                def _(s):
                    if t == 0:
                        for _j in range(4):
                            if poutS[_j].n:
                                s.wait_ge(poutS[_j].h, poutS[_j].n)
                    def xload(i):
                        kind, idx = subs[i]
                        srcx = halo_r[idx] if kind == "halo" else x_r[t * TS + idx]
                        xS[i % 2].inc(s.dma_start(xslot[i % 2], srcx), 16)
                        xtot[i] = xS[i % 2].n
                    xload(0)
                    xload(1)
                    nr = 8 if t == 0 else 4
                    r0_ = 0 if t == 0 else 4 + t * TS
                    sl = 0 if t == 0 else 4
                    csS[0].inc(s.dma_start(cos_sb[:, sl:sl + nr, :],
                                           cos_in[r0_:r0_ + nr].rearrange("n p f -> p n f")), 16)
                    cstot[0] = csS[0].n
                    csS[1].inc(s.dma_start(sin_sb[:, sl:sl + nr, :],
                                           sin_in[r0_:r0_ + nr].rearrange("n p f -> p n f")), 16)
                    cstot[1] = csS[1].n
                    for kc in range(KC):
                        if t == 0:
                            wait_piece(s, ("wkv", kc))
                        slabS[kc].inc(s.dma_start(wkv_slab[kc], c_wkv[kc]), 16)
                        stot[("kv", kc)] = slabS[kc].n
                    for kc in range(KC):
                        if t == 0:
                            wait_piece(s, ("wq", kc))
                        qslabS[kc].inc(s.dma_start(wq_slab[kc], c_wq[kc]), 16)
                        stot[("q", kc)] = qslabS[kc].n
                    for i in range(2, len(subs)):
                        s.wait_ge(sV.h, plan[("v3", i - 2)])
                        xload(i)

                @blk.tensor
                def _(te):
                    for i, (kind, idx) in enumerate(subs):
                        key = ("h", idx) if kind == "halo" else idx
                        nb = plan[("nb", i)]
                        for g in range(nb):
                            ptg(te, plan[("NtrG", i, g)])
                            if g == 0:
                                te.wait_ge(sV.h, plan[("v3" if kind == "halo" else "v2", i)])
                            if kind == "main" and g == 4:
                                te.wait_ge(sV.h, plan[("v3", i)])
                            src_tm = xn_tm if (kind == "main" and g < 4) else xn2_tm
                            ch0 = (g % 4) * 4
                            for ci in range(4):
                                ch = ch0 + ci
                                ins = te.matmul(pT[plan[("NtrG", i, g)] % 2][:, ci * 128:(ci + 1) * 128],
                                                src_tm[:, ch * 128:(ch + 1) * 128],
                                                identb[:, :], start=True, stop=True)
                            sT.inc(ins)
                        if kind == "main":
                            ts = idx
                            te.wait_ge(sE.h, plan[("Ne", i, nb - 1)])
                            if i >= 1:
                                pk, pi = subs[i - 1]
                                te.wait_ge(sV.h, plan[("vke", ("h", pi) if pk == "halo" else pi)])
                            elif t > 0:
                                pass
                            for kc in range(KC):
                                if ts == 0:
                                    te.wait_ge(qslabS[kc].h, stot[("q", kc)])
                                lhsT = xn2_fm[:, kc, ts * 128:(ts + 1) * 128]
                                for dc in range(4):
                                    ins = te.matmul(Bk[dc][:, :], lhsT,
                                                    wq_slab[kc][:, dc * 512:(dc + 1) * 512],
                                                    start=(kc == 0), stop=(kc == KC - 1))
                            sT.inc(ins)
                            te.wait_ge(sV.h, plan[("vqe", ts)])
                            for kc in range(KC):
                                if ts == 0 and t > 0:
                                    te.wait_ge(slabS[kc].h, stot[("kv", kc)])
                                lhsT = xn2_fm[:, kc, ts * 128:(ts + 1) * 128]
                                te.matmul(Bk[0][:, :], lhsT, wkv_slab[kc][:, 0:512],
                                          start=(kc == 0), stop=(kc == KC - 1))
                                ins = te.matmul(Bk[1][:, :], lhsT, wkv_slab[kc][:, 512:1024],
                                                start=(kc == 0), stop=(kc == KC - 1))
                            sT.inc(ins)
                            te.wait_ge(sV.h, plan[("vqr", ts)])
                            for g in range(4):
                                ptg(te, plan[("QtrG", ts, g)])
                                for hi in range(4):
                                    h = g * 4 + hi
                                    ins = te.matmul(pT[plan[("QtrG", ts, g)] % 2][:, hi * 128:(hi + 1) * 128],
                                                    qrot[:, h * 128:(h + 1) * 128],
                                                    identb[:, :], start=True, stop=True)
                                sT.inc(ins)
                        else:
                            te.wait_ge(sE.h, plan[("Ne", i, nb - 1)])
                            if i >= 1:
                                te.wait_ge(sV.h, plan[("vke", ("h", idx - 1))])
                            for kc in range(KC):
                                if idx == 0:
                                    te.wait_ge(slabS[kc].h, stot[("kv", kc)])
                                lhsT = halo_xn2[:, kc, idx * 128:(idx + 1) * 128]
                                te.matmul(Bk[0][:, :], lhsT, wkv_slab[kc][:, 0:512],
                                          start=(kc == 0), stop=(kc == KC - 1))
                                ins = te.matmul(Bk[1][:, :], lhsT, wkv_slab[kc][:, 512:1024],
                                                start=(kc == 0), stop=(kc == KC - 1))
                            sT.inc(ins)
                        te.wait_ge(sV.h, plan[("vkr", key)])
                        ptg(te, plan[("KVtrG", key)])
                        for ci in range(4):
                            ins = te.matmul(pT[plan[("KVtrG", key)] % 2][:, ci * 128:(ci + 1) * 128],
                                            krot[ci * 128:(ci + 1) * 128] if False else
                                            krot[:, ci * 128:(ci + 1) * 128],
                                            identb[:, :], start=True, stop=True)
                        sT.inc(ins)
                    assert sT.n == tc

                @blk.vector
                def _(v):
                    v.wait_ge(csS[0].h, cstot[0])
                    v.wait_ge(csS[1].h, cstot[1])

                    def nchain(i):
                        kind, idx = subs[i]
                        xb = xslot[i % 2]
                        v.wait_ge(xS[i % 2].h, xtot[i])
                        if i >= 1:
                            v.wait_ge(sS.h, plan[("ssq", i - 1)])
                        ve(v, v.reduce_sum(sv(CSUM), xb, axis=AX.X))
                        sV.inc(v.tensor_scalar(sv(CNM), sv(CSUM), -INV_D, None, ALU.mult))
                        v.wait_ge(sS.h, plan[("ssq", i)])
                        sV.inc(v.tensor_tensor(sv(CVAR), sv(CVA), sv(CVB), ALU.add))
                        v.wait_ge(sS.h, plan[("s1", i)])
                        if kind == "main":
                            v.tensor_scalar(sv(CM2), sv(CIVE), sv(CVAR), INV_D,
                                            ALU.mult, ALU.mult)
                            if i >= 1:
                                v.wait_ge(sT.h, plan[("Ntr", i - 1, plan[("nb", i - 1)] - 1)])
                            sV.inc(v.tensor_scalar(xn_tm[:, :], xb, sv(CNM), sv(CR1),
                                                   ALU.add, ALU.mult))
                        else:
                            sV.inc(v.tensor_scalar(sv(CM2), sv(CIVE), sv(CVAR), INV_D,
                                                   ALU.mult, ALU.mult))
                        v.wait_ge(sS.h, plan[("s2", i)])
                        ve(v, v.tensor_tensor(sv(CR2), sv(CR1), sv(CC2), ALU.mult))
                        if i >= 1:
                            v.wait_ge(sT.h, plan[("Ntr", i - 1, plan[("nb", i - 1)] - 1)])
                        sV.inc(v.tensor_scalar(xn2_tm[:, :], xb, sv(CNM), sv(CR2),
                                               ALU.add, ALU.mult))

                    def qkv(i):
                        kind, idx = subs[i]
                        key = ("h", idx) if kind == "halo" else idx
                        if kind == "main":
                            ts = idx
                            v.wait_ge(sT.h, plan[("Qmm", ts)])
                            for dc in range(4):
                                ins = v.tensor_copy(qtm[:, dc * 512:(dc + 1) * 512], Bk[dc][:, :])
                            sV.inc(ins)
                            v.wait_ge(sV.h, sV.n)
                            qt3 = qtm.rearrange("p (h f) -> p h f", h=H)
                            qr3 = qrot.rearrange("p (h f) -> p h f", h=H)
                            c3 = cos_sb[:, 4 + ts, :].unsqueeze(1).broadcast_to([128, 8, 64])
                            s3 = sin_sb[:, 4 + ts, :].unsqueeze(1).broadcast_to([128, 8, 64])
                            rA = ropeA.rearrange("p (h f) -> p h f", h=8)
                            rB = ropeB.rearrange("p (h f) -> p h f", h=8)
                            for hh in range(2):
                                hs_ = slice(hh * 8, (hh + 1) * 8)
                                v.tensor_tensor(rA, qt3[:, hs_, 0:64], c3, ALU.mult)
                                ve(v, v.tensor_tensor(rB, qt3[:, hs_, 64:128], s3, ALU.mult))
                                v.tensor_tensor(qr3[:, hs_, 0:64], rA, rB, ALU.subtract)
                                v.tensor_tensor(rA, qt3[:, hs_, 0:64], s3, ALU.mult)
                                ve(v, v.tensor_tensor(rB, qt3[:, hs_, 64:128], c3, ALU.mult))
                                ins = v.tensor_tensor(qr3[:, hs_, 64:128], rA, rB, ALU.add)
                            sV.inc(ins)
                        slot_ = cur if kind == "main" else prv
                        csl = (4 + idx) if kind == "main" else idx
                        v.wait_ge(sT.h, plan[("KVmm", key)])
                        if kind == "main" and idx >= 1:
                            v.wait_ge(sT.h, plan[("KVtr", idx - 1)])
                        elif kind == "main" and t == 0:
                            v.wait_ge(sT.h, plan[("KVtr", ("h", 3))])
                        elif kind == "halo" and idx >= 1:
                            v.wait_ge(sT.h, plan[("KVtr", ("h", idx - 1))])
                        v.tensor_copy(ktm, Bk[0][:, :])
                        sV.inc(v.tensor_copy(v_ring[:, slot_, idx, :], Bk[1][:, :]))
                        v.wait_ge(sV.h, sV.n)
                        kt = ktm.rearrange("p (h f) -> p h f", h=KH)
                        kr = krot.rearrange("p (h f) -> p h f", h=KH)
                        c4 = cos_sb[:, csl, :].unsqueeze(1).broadcast_to([128, KH, 64])
                        s4 = sin_sb[:, csl, :].unsqueeze(1).broadcast_to([128, KH, 64])
                        mA_ = kmA.rearrange("p (h f) -> p h f", h=KH)
                        mB_ = kmB.rearrange("p (h f) -> p h f", h=KH)
                        v.tensor_tensor(mA_, kt[:, :, 0:64], c4, ALU.mult)
                        ve(v, v.tensor_tensor(mB_, kt[:, :, 64:128], s4, ALU.mult))
                        v.tensor_tensor(kr[:, :, 0:64], mA_, mB_, ALU.subtract)
                        v.tensor_tensor(mA_, kt[:, :, 0:64], s4, ALU.mult)
                        ve(v, v.tensor_tensor(mB_, kt[:, :, 64:128], c4, ALU.mult))
                        sV.inc(v.tensor_tensor(kr[:, :, 64:128], mA_, mB_, ALU.add))

                    for ev, i in vorder:
                        if ev == "N":
                            nchain(i)
                        else:
                            qkv(i)
                    assert sV.n == vc

                @blk.scalar
                def _(sc):
                    if o_guard[0] or o_guard[1]:
                        sc.wait_ge(sO[0].h, o_guard[0])
                        sc.wait_ge(sO[1].h, o_guard[1])

                    def chain1(i):
                        kind, idx = subs[i]
                        sc.wait_ge(sV.h, plan[("v1", i)])
                        se(sc, sc.activation(sqdump[:, :], xslot[i % 2][:, 0:1024],
                                             AF.Square, bias=sv(CNM), accum_out=sv(CVA)))
                        sS.inc(sc.activation(sqdump[:, :], xslot[i % 2][:, 1024:2048],
                                             AF.Square, bias=sv(CNM), accum_out=sv(CVB)))
                        sc.wait_ge(sV.h, plan[("vsq", i)])
                        se(sc, sc.activation(sv(CL1), sv(CVAR), AF.Ln, scale=INV_D,
                                             bias=constv[:, 0:1]))
                        sc.activation(sv(CR1), sv(CL1), AF.Exp, scale=-0.5)
                        sS.inc(sc.activation(sv(CIVE), sv(CL1), AF.Exp, scale=-1.0))

                    def chain2(i):
                        sc.wait_ge(sV.h, plan[("v2", i)])
                        se(sc, sc.activation(sv(CL2), sv(CM2), AF.Ln, bias=constv[:, 1:2]))
                        sS.inc(sc.activation(sv(CC2), sv(CL2), AF.Exp, scale=-0.5))

                    def evacsN(i):
                        kind, idx = subs[i]
                        nb = plan[("nb", i)]
                        for g in range(nb):
                            sc.wait_ge(sT.h, plan[("Ntr", i, g)])
                            ch0 = (g % 4) * 4
                            if kind == "main":
                                dstt = (xn_fm if g < 4 else xn2_fm)[:, ch0:ch0 + 4,
                                                                    idx * 128:(idx + 1) * 128]
                            else:
                                dstt = halo_xn2[:, ch0:ch0 + 4, idx * 128:(idx + 1) * 128]
                            srcp = pT[plan[("NtrG", i, g)] % 2][:, :].rearrange("p (a b) -> p a b", a=4)
                            sE.inc(sc.copy(dstt, srcp))

                    def evacsQK(i):
                        kind, idx = subs[i]
                        key = ("h", idx) if kind == "halo" else idx
                        if kind == "main":
                            for g in range(4):
                                sc.wait_ge(sT.h, plan[("Qtr", idx, g)])
                                dstt = q_fm[:, g * 4:(g + 1) * 4, idx * 128:(idx + 1) * 128]
                                srcp = pT[plan[("QtrG", idx, g)] % 2][:, :].rearrange("p (a b) -> p a b", a=4)
                                sE.inc(sc.copy(dstt, srcp))
                        sc.wait_ge(sT.h, plan[("KVtr", key)])
                        slot_ = cur if kind == "main" else prv
                        dstt = k_ring[:, slot_, :, idx * 128:(idx + 1) * 128]
                        srcp = pT[plan[("KVtrG", key)] % 2][:, :].rearrange("p (a b) -> p a b", a=4)
                        sE.inc(sc.copy(dstt, srcp))

                    for i in range(len(subs)):
                        chain1(i)
                        if i >= 1:
                            evacsN(i - 1)
                        chain2(i)
                        if i >= 1:
                            evacsQK(i - 1)
                    evacsN(len(subs) - 1)
                    evacsQK(len(subs) - 1)
                    assert sS.n == scc and sE.n == ec


            with nc.Block(f"A{t}") as blk:
                t0_, v0, s0 = sT.n, sV.n, sS.n
                mtot = [mS.n]


                @blk.sync
                def _(s):
                    if t == 0:
                        mS.inc(s.dma_start(mT0[:, :, :],
                                           mt0_in.rearrange("n p f -> p n f")), 16)
                        mtot[0] = mS.n

                def t_scores(a, hu):
                    return t0_ + 10 * a + hu + 1

                def t_trb(a, j):
                    return t0_ + 10 * a + 4 + j + 1

                def t_av(a):
                    return t0_ + 10 * (a + 1)

                def v_bias(a, hu):
                    return v0 + 9 * a + 2 * hu + 1

                def v_norm(a, hu):
                    return v0 + 9 * a + 2 * hu + 2

                def v_aev(a):
                    return v0 + 9 * (a + 1)

                def s_exp(a, hu):
                    return s0 + 9 * a + hu + 1

                def s_pev(a, j):
                    return s0 + 9 * a + 4 + j + 1

                units = [(kg, qt) for kg in range(KH) for qt in range(TS)]

                @blk.tensor
                def _(te):
                    for a, (kg, qt) in enumerate(units):
                        par = a % 2
                        N1 = 512 - qt * 128
                        N2 = 128 + qt * 128
                        for hu in range(4):
                            h = kg * 4 + hu
                            u_abs = 4 * a + hu
                            if u_abs >= 2:
                                prev_a, prev_h = divmod(u_abs - 2, 4)
                                te.wait_ge(sS.h, s_exp(prev_a, prev_h))
                            pp = hu % 2
                            SAB = pSAB[pp]
                            lq = q_fm[:, h, qt * 128:(qt + 1) * 128]
                            te.matmul(SAB[:, 512 - N1:512], lq,
                                      k_ring[:, prv, kg, qt * 128:512],
                                      start=True, stop=True)
                            ins = te.matmul(SAB[:, 512:512 + N2], lq,
                                            k_ring[:, cur, kg, 0:N2],
                                            start=True, stop=True)
                            sT.inc(ins)
                        te.wait_ge(sV.h, v_norm(a, 3))
                        if a >= 1:
                            pass
                        for j in range(5):
                            gb = 5 * a + j
                            if gb >= 2:
                                pa, pj = divmod(gb - 2, 5)
                                te.wait_ge(sS.h, s_pev(pa, pj))
                            for hu in range(4):
                                ins = te.matmul(pT[gb % 2][:, hu * 128:(hu + 1) * 128],
                                                probs_n[par][hu][:, j * 128:(j + 1) * 128],
                                                identb[:, :], start=True, stop=True)
                            sT.inc(ins)
                        te.wait_ge(sS.h, s_pev(a, 4))
                        if a >= 2:
                            te.wait_ge(sV.h, v_aev(a - 2))
                        AVb = Bk[4 + par]
                        for j in range(5):
                            wcol = qt + j
                            slot, st = (prv, wcol) if wcol < 4 else (cur, wcol - 4)
                            ins = te.matmul(AVb[:, :],
                                            v_ring[:, slot, st, kg * 128:(kg + 1) * 128],
                                            probsT[par][:, j, :, :].rearrange("p a b -> p (a b)"),
                                            start=(j == 0), stop=(j == 4))
                        sT.inc(ins)

                @blk.vector
                def _(v):
                    if t == 0:
                        v.wait_ge(mS.h, mtot[0])
                    for a, (kg, qt) in enumerate(units):
                        par = a % 2
                        N1 = 512 - qt * 128
                        N2 = 128 + qt * 128
                        for hu in range(4):
                            v.wait_ge(sT.h, t_scores(a, hu))
                            pp = hu % 2
                            SAB = pSAB[pp]
                            strip = SAB[:, 512 - N1:512 + N2]
                            if t == 0:
                                ve(v, v.tensor_tensor(strip, strip, mT0[:, qt, :], ALU.add))
                            else:
                                v.tensor_tensor(SAB[:, 512 - N1:512 - N1 + 128],
                                                SAB[:, 512 - N1:512 - N1 + 128],
                                                mP1[:, :], ALU.add)
                                ve(v, v.tensor_tensor(SAB[:, 512 + N2 - 128:512 + N2],
                                                      SAB[:, 512 + N2 - 128:512 + N2],
                                                      mP2[:, :], ALU.add))
                            ve(v, v.reduce_max(smx(pp, 2), strip, axis=AX.X, negate=True))
                            sV.inc(v.tensor_scalar(smx(pp, 3), smx(pp, 2), SCALE,
                                                   None, ALU.mult))
                            v.wait_ge(sS.h, s_exp(a, hu))
                            ve(v, v.reciprocal(smx(pp, 7), smx(pp, 6)))
                            sV.inc(v.tensor_scalar(probs_n[par][hu][:, :],
                                                   probs_n[par][hu][:, :],
                                                   smx(pp, 7), None, ALU.mult))
                        v.wait_ge(sT.h, t_av(a))
                        dstt = attnout[:, kg * 4:(kg + 1) * 4, qt * 128:(qt + 1) * 128]
                        sV.inc(v.tensor_copy(dstt,
                                             Bk[4 + par][:, :].rearrange("p (a b) -> p a b", a=4)))

                @blk.scalar
                def _(sc):
                    for a, (kg, qt) in enumerate(units):
                        par = a % 2
                        N1 = 512 - qt * 128
                        N2 = 128 + qt * 128
                        for hu in range(4):
                            sc.wait_ge(sV.h, v_bias(a, hu))
                            pp = hu % 2
                            SAB = pSAB[pp]
                            sS.inc(sc.activation(probs_n[par][hu][:, :],
                                                 SAB[:, 512 - N1:512 + N2], AF.Exp,
                                                 bias=smx(pp, 3), scale=SCALE,
                                                 accum_out=smx(pp, 6)))
                        for j in range(5):
                            sc.wait_ge(sT.h, t_trb(a, j))
                            sS.inc(sc.copy(probsT[par][:, j, :, :],
                                                  pT[(5 * a + j) % 2][:, :].rearrange(
                                                      "p (a b) -> p a b", a=4)))

                assert sT.n == t0_ + 160 and sV.n == v0 + 144 and sS.n == s0 + 144

            with nc.Block(f"O{t}") as blk:
                t0_, v0 = sT.n, sV.n
                wotot = {}


                @blk.sync
                def _(s):
                    for oc in range(KC):
                        if oc >= 2:
                            s.wait_ge(sT.h, t0_ + oc - 1)
                        if t == 0:
                            wait_piece(s, ("wo", oc))
                        woS[oc % 2].inc(s.dma_start(woblk[:, oc % 2, :, :], c_wo[oc]), 16)
                        wotot[oc] = woS[oc % 2].n

                @blk.tensor
                def _(te):
                    for oc in range(KC):
                        te.wait_ge(woS[oc % 2].h, wotot[oc])
                        if oc >= 2:
                            te.wait_ge(sV.h, v0 + oc - 1)
                        for kc in range(KC):
                            ins = te.matmul(Bk[oc % 2][:, :], woblk[:, oc % 2, kc, :],
                                            attnout[:, kc, :],
                                            start=(kc == 0), stop=(kc == KC - 1))
                        sT.inc(ins)

                @blk.vector
                def _(v):
                    for oc in range(KC):
                        v.wait_ge(sT.h, t0_ + oc + 1)
                        sV.inc(v.tensor_copy(s_fm[:, oc, :], Bk[oc % 2][:, :]))

            with nc.Block(f"F{t}") as blk:
                t0_, v0, s0 = sT.n, sV.n, sS.n
                wgtot, wutot, wdtot = {}, {}, {}

                tg = lambda oc: t0_ + 2 * oc + 1
                tu = lambda oc: t0_ + 2 * oc + 2
                sg_ = lambda oc: s0 + oc + 1
                vu = lambda oc: v0 + 2 * oc + 1
                vf = lambda oc: v0 + 2 * oc + 2
                td = lambda ocd, q: t0_ + 128 + 4 * ocd + q + 1
                sd_ = lambda ocd: s0 + 64 + ocd + 1
                vsg = lambda ocd: v0 + 128 + ocd + 1

                @blk.sync
                def _(s):
                    for oc in range(OCI):
                        if oc >= 2:
                            s.wait_ge(sT.h, tg(oc - 2))
                        if t == 0:
                            wait_piece(s, ("wg", oc))
                        wgS[oc % 2].inc(s.dma_start(wgblk[:, oc % 2, :, :], c_wg[oc]), 16)
                        wgtot[oc] = wgS[oc % 2].n
                        if oc >= 1:
                            s.wait_ge(sT.h, tu(oc - 1))
                        if t == 0:
                            wait_piece(s, ("wu", oc))
                        wuS.inc(s.dma_start(wublk[:, :, :], c_wu[oc]), 16)
                        wutot[oc] = wuS.n
                    for ocd in range(KC):
                        for q in range(4):
                            gq = ocd * 4 + q
                            if gq >= 3:
                                pocd, pq = divmod(gq - 3, 4)
                                s.wait_ge(sT.h, td(pocd, pq))
                            if t == 0:
                                wait_piece(s, ("wd", ocd, q))
                            wdS[gq % 3].inc(s.dma_start(wdblk[:, (gq % 3) * 16:(gq % 3 + 1) * 16, :],
                                                        c_wd[ocd][:, q * 16 * 128:(q + 1) * 16 * 128]), 16)
                            wdtot[(ocd, q)] = wdS[gq % 3].n

                @blk.tensor
                def _(te):
                    for oc in range(OCI):
                        te.wait_ge(wgS[oc % 2].h, wgtot[oc])
                        if oc >= 2:
                            te.wait_ge(sS.h, sg_(oc - 2))
                        for kc in range(KC):
                            ins = te.matmul(Bk[oc % 2][:, :], wgblk[:, oc % 2, kc, :],
                                            xn_fm[:, kc, :],
                                            start=(kc == 0), stop=(kc == KC - 1))
                        sT.inc(ins)
                        te.wait_ge(wuS.h, wutot[oc])
                        if oc >= 2:
                            te.wait_ge(sV.h, vu(oc - 2))
                        for kc in range(KC):
                            ins = te.matmul(Bk[2 + oc % 2][:, :], wublk[:, kc, :],
                                            xn_fm[:, kc, :],
                                            start=(kc == 0), stop=(kc == KC - 1))
                        sT.inc(ins)
                    for ocd in range(KC):
                        if ocd >= 2:
                            te.wait_ge(sS.h, sd_(ocd - 2))
                        for q in range(4):
                            gq = ocd * 4 + q
                            te.wait_ge(wdS[gq % 3].h, wdtot[(ocd, q)])
                            if ocd == 0:
                                te.wait_ge(sV.h, vf((q + 1) * 16 - 1))
                            for kc in range(q * 16, (q + 1) * 16):
                                ins = te.matmul(Bk[4 + ocd % 2][:, :],
                                                wdblk[:, (gq % 3) * 16 + (kc - q * 16), :],
                                                fuse[:, kc, :],
                                                start=(kc == 0), stop=(kc == OCI - 1))
                            sT.inc(ins)

                @blk.scalar
                def _(sc):
                    for oc in range(OCI):
                        sc.wait_ge(sT.h, tg(oc))
                        if oc >= 2:
                            sc.wait_ge(sV.h, vf(oc - 2))
                        sS.inc(sc.activation(gate_oc[:, oc % 2, :], Bk[oc % 2][:, :],
                                             AF.Silu, scale=sgate_t[:, oc:oc + 1]))
                    for ocd in range(KC):
                        sc.wait_ge(sT.h, td(ocd, 3))
                        if ocd >= 2:
                            sc.wait_ge(sV.h, vsg(ocd - 2))
                        sS.inc(sc.activation(g_oc[:, ocd % 2, :], Bk[4 + ocd % 2][:, :],
                                             AF.Silu, scale=sdown_t[:, ocd:ocd + 1]))

                @blk.vector
                def _(v):
                    for oc in range(OCI):
                        v.wait_ge(sT.h, tu(oc))
                        sV.inc(v.tensor_scalar(up_oc[:, oc % 2, :], Bk[2 + oc % 2][:, :],
                                               sup_t[:, oc:oc + 1], None, ALU.mult))
                        v.wait_ge(sV.h, sV.n)
                        v.wait_ge(sS.h, sg_(oc))
                        sV.inc(v.tensor_tensor(fuse[:, oc, :], gate_oc[:, oc % 2, :],
                                               up_oc[:, oc % 2, :], ALU.mult))
                    for ocd in range(KC):
                        v.wait_ge(sS.h, sd_(ocd))
                        sV.inc(v.tensor_tensor(sg_fm[:, ocd, :], s_fm[:, ocd, :],
                                               g_oc[:, ocd % 2, :], ALU.mult))

                assert sT.n == t0_ + 192 and sS.n == s0 + 80 and sV.n == v0 + 144

            with nc.Block(f"E{t}") as blk:
                t0_, v0 = sT.n, sV.n
                o0 = (sO[0].n, sO[1].n)
                xtot = {}

                @blk.scalar
                def _(sc):
                    for ts in range(TS):
                        if ts >= 2:
                            sc.wait_ge(sV.h, v0 + 4 * (ts - 1))
                        xS[ts % 2].inc(sc.dma_start(xslot[ts % 2], x_r[t * TS + ts]), 16)
                        xtot[ts] = xS[ts % 2].n

                @blk.tensor
                def _(te):
                    for ts in range(TS):
                        for g in range(4):
                            G = 4 * ts + g
                            if G >= 2:
                                te.wait_ge(sV.h, v0 + G - 1)
                            for i in range(4):
                                oc = g * 4 + i
                                ins = te.matmul(pT[G % 2][:, i * 128:(i + 1) * 128],
                                                sg_fm[:, oc, ts * 128:(ts + 1) * 128],
                                                identb[:, :], start=True, stop=True)
                            sT.inc(ins)

                @blk.vector
                def _(v):
                    for ts in range(TS):
                        v.wait_ge(xS[ts % 2].h, xtot[ts])
                        if ts >= 2:
                            v.wait_ge(sO[ts % 2].h, o0[ts % 2] + 16)
                        for g in range(4):
                            G = 4 * ts + g
                            v.wait_ge(sT.h, t0_ + G + 1)
                            sV.inc(v.tensor_tensor(out_stage[ts % 2][:, g * 512:(g + 1) * 512],
                                                   pT[G % 2][:, :],
                                                   xslot[ts % 2][:, g * 512:(g + 1) * 512],
                                                   ALU.add))

                @blk.sync
                def _(s):
                    for ts in range(TS):
                        s.wait_ge(sV.h, v0 + 4 * (ts + 1))
                        sO[ts % 2].inc(s.dma_start(out_r[t * TS + ts], out_stage[ts % 2]), 16)

        with nc.Block("fin") as blk:
            fin = (sO[0].n, sO[1].n)

            @blk.sync
            def _(s):
                s.wait_ge(sO[0].h, fin[0])
                s.wait_ge(sO[1].h, fin[1])

    return nc


_PROG_CACHE = {}


def _get_program():
    if "nc" not in _PROG_CACHE:
        _PROG_CACHE["nc"] = build_program()
    return _PROG_CACHE["nc"]


def _rope_perm(nheads):
    perm = []
    for h in range(nheads):
        base = h * 128
        perm += [base + 2 * i for i in range(64)]
        perm += [base + 2 * i + 1 for i in range(64)]
    return np.array(perm, dtype=np.int64)


def kernel(L0_attn_x, L0_attn_ln_w, L0_attn_wq, L0_attn_wk, L0_attn_wv, L0_attn_wo,
           L0_ffn_w_gate, L0_ffn_w_up, L0_ffn_w_down, L0_ffn_s_gate, L0_ffn_s_up,
           L0_ffn_s_down):
    x = np.ascontiguousarray(np.asarray(L0_attn_x, dtype=np.float32))
    wq = np.ascontiguousarray(np.asarray(L0_attn_wq, dtype=np.float32)[:, _rope_perm(H)])
    wk = np.ascontiguousarray(np.asarray(L0_attn_wk, dtype=np.float32)[:, _rope_perm(KH)])
    wv = np.ascontiguousarray(np.asarray(L0_attn_wv, dtype=np.float32))
    wo = np.ascontiguousarray(np.asarray(L0_attn_wo, dtype=np.float32))
    wg = np.ascontiguousarray(np.asarray(L0_ffn_w_gate, dtype=np.int8))
    wu = np.ascontiguousarray(np.asarray(L0_ffn_w_up, dtype=np.int8))
    wd = np.ascontiguousarray(np.asarray(L0_ffn_w_down, dtype=np.int8))
    sgate = np.ascontiguousarray(np.asarray(L0_ffn_s_gate, dtype=np.float32).reshape(OCI, 128).T)
    sup = np.ascontiguousarray(np.asarray(L0_ffn_s_up, dtype=np.float32).reshape(OCI, 128).T)
    sdown = np.ascontiguousarray(np.asarray(L0_ffn_s_down, dtype=np.float32).reshape(KC, 128).T)
    lnw = np.ascontiguousarray(np.asarray(L0_attn_ln_w, dtype=np.float32).reshape(KC, 128).T)

    qi = np.arange(128)[:, None]
    cc = np.arange(128)[None, :]
    mP1 = np.where(qi <= cc, 0.0, NEG).astype(np.float32)
    mP2 = np.where(cc <= qi, 0.0, NEG).astype(np.float32)
    identm = np.eye(128, dtype=ml_dtypes.bfloat16)

    inv_freq = (1.0 / 10000.0 ** (np.arange(0, HD, 2, dtype=np.float32) / HD)).astype(np.float32)

    def trig_tables(seq_off):
        pos = np.concatenate([np.arange(seq_off - W, seq_off), np.arange(seq_off, seq_off + SC)])
        pos = np.maximum(pos, 0).astype(np.float32)
        ang = pos[:, None] * inv_freq[None, :]
        return (np.cos(ang).astype(ml_dtypes.bfloat16).reshape(-1, 128, 64),
                np.sin(ang).astype(ml_dtypes.bfloat16).reshape(-1, 128, 64))

    def t0_mask(first_half):
        m = np.empty((4, 128, 640), dtype=np.float32)
        for qt in range(4):
            cp = np.arange(640)[None, :]
            valid = (qi <= cp) & (cp <= qi + 512)
            if first_half:
                valid &= (qt * 128 + cp) >= 512
            m[qt] = np.where(valid, 0.0, NEG)
        return m

    nc = _get_program()
    in_maps = []
    for c in range(NCORE):
        b, half = c // 2, c % 2
        off = half * SC
        xc = np.ascontiguousarray(x[b, off:off + SC])
        haloc = (np.zeros((W, D), dtype=np.float32) if half == 0
                 else np.ascontiguousarray(x[b, off - W:off]))
        cosb, sinb = trig_tables(off)
        in_maps.append({
            "x": xc, "halo": haloc, "wq": wq, "wk": wk, "wv": wv, "wo": wo,
            "wg": wg, "wu": wu, "wd": wd, "sgate": sgate, "sup": sup,
            "sdown": sdown, "lnw": lnw, "cosb": cosb, "sinb": sinb,
            "mP1": mP1, "mP2": mP2, "mT0": t0_mask(half == 0), "identb": identm,
        })

    res = run_bass_kernel_spmd(nc, in_maps, core_ids=list(range(NCORE)))
    out = np.empty((B, S, D), dtype=np.float32)
    for c in range(NCORE):
        b, half = c // 2, c % 2
        out[b, half * SC:(half + 1) * SC] = res.results[c]["out"]
    return out
